# version 03 (final)
# baseline (speedup 1.0000x reference)
import contextlib
import numpy as np
import ml_dtypes

import concourse.bass as bass
import concourse.mybir as mybir
from concourse.bass_utils import run_bass_kernel_spmd

F32 = mybir.dt.float32
BF16 = mybir.dt.bfloat16
ALU = mybir.AluOpType
ACT = mybir.ActivationFunctionType
AX = mybir.AxisListType.X
BF = ml_dtypes.bfloat16

NCORE = 8
S = 8192
D = 4096
TPC = S // NCORE
HD = 1024
EPS = 1e-5
R4 = 1.0 / 4096.0

TAPS = []
PHASE_LIMIT = "F"
_ORDER = "ABCDEF"
import os as _os
STRICT = int(_os.environ.get("BASS_STRICT", "1"))


def _ph(p):
    return _ORDER.index(p) <= _ORDER.index(PHASE_LIMIT)



def _hilo_const(m):
    hi = m.astype(BF)
    lo = (m.astype(np.float32) - hi.astype(np.float32)).astype(BF)
    return hi, lo


def _pad128(m, dtype):
    out = np.zeros((128,) + m.shape[1:], dtype=dtype)
    out[: m.shape[0]] = m
    return out


def host_prep(inputs):
    f32 = np.float32
    g = lambda k: np.asarray(inputs[k], dtype=f32)
    x = np.ascontiguousarray(g("L0_attn_x").reshape(S, D))
    wqkv = g("L0_attn_Wqkv")
    bqkv = g("L0_attn_bqkv")
    base = {}
    base["xT"] = np.ascontiguousarray(x.T)
    base["xf"] = x
    base["wq"] = np.ascontiguousarray(wqkv[:, 0:HD])
    base["wk"] = np.ascontiguousarray(wqkv[:, HD:2 * HD])
    base["wv"] = np.ascontiguousarray(wqkv[:, 2 * HD:3 * HD])
    base["bq_cols"] = np.ascontiguousarray(bqkv[0:HD].reshape(8, 128).T)
    base["bv_cols"] = np.ascontiguousarray(bqkv[2 * HD:3 * HD].reshape(8, 128).T)
    base["bo_row"] = np.ascontiguousarray(g("L0_attn_bo").reshape(1, D))
    base["wo"] = g("L0_attn_Wo")
    base["rwT"] = np.ascontiguousarray(g("L0_ffn_route_w").T)
    base["aT"] = np.ascontiguousarray(g("L0_ffn_A").transpose(2, 0, 1).reshape(D, 128))
    base["bstk"] = np.ascontiguousarray(g("L0_ffn_Bw").transpose(0, 2, 1).reshape(128, D))
    base["gate_w"] = g("L1_ffn_gate_w")
    base["up_w"] = g("L1_ffn_up_w")
    base["down_w"] = g("L1_ffn_down_w")
    base["fc1_w"] = g("L1_ffn_fc1_w")
    base["fc2_w"] = g("L1_ffn_fc2_w")
    base["exp_w"] = g("L1_ffn_expert_w")
    base["gam_row"] = np.ascontiguousarray(g("L1_ffn_ln_gamma").reshape(1, D))
    base["bet_row"] = np.ascontiguousarray(g("L1_ffn_ln_beta").reshape(1, D))
    lam = g("L1_attn_Lambda_ri")
    base["lr_c"] = np.ascontiguousarray(lam[:, 0:1])
    base["li_c"] = np.ascontiguousarray(lam[:, 1:2])
    pri = g("L1_attn_p_ri")
    base["pr_c"] = np.ascontiguousarray(pri[:, 0:1])
    base["pi_c"] = np.ascontiguousarray(pri[:, 1:2])
    qri = g("L1_attn_q_ri")
    base["qr_c"] = np.ascontiguousarray(qri[:, 0:1])
    base["qi_c"] = np.ascontiguousarray(qri[:, 1:2])
    ct = g("L1_attn_Ct_ri").reshape(64, 2)
    base["ctr_c"] = np.ascontiguousarray(ct[:, 0:1])
    base["cti_c"] = np.ascontiguousarray(ct[:, 1:2])
    vc = g("L1_attn_Vc_ri")
    base["vcrT"] = np.ascontiguousarray(vc[:, :, 0].T)
    base["vciT"] = np.ascontiguousarray(vc[:, :, 1].T)
    base["bp_c"] = np.ascontiguousarray(g("L1_attn_Bp").reshape(64, 1))
    base["dd"] = np.ascontiguousarray(g("L1_attn_D").reshape(1, 1))
    base["lstep"] = np.ascontiguousarray(g("L1_attn_log_step").reshape(1, 1))

    base["identf"] = np.eye(128, dtype=f32)
    base["identb"] = np.eye(128, dtype=BF)
    base["jrev"] = np.ascontiguousarray(np.eye(128, dtype=BF)[::-1])
    in64 = np.zeros((64, 64, 64), f32)
    for kk in range(64):
        in64[kk, :, kk] = 1.0
    base["in64"] = _pad128(in64.reshape(64, 4096), BF)
    e1 = np.zeros((64, 64, 64), f32)
    for ll in range(64):
        e1[ll, ll, :] = 1.0
    base["e1c"] = _pad128(e1.reshape(64, 4096), BF)
    ab = np.outer(np.arange(64), np.arange(64))
    w64 = np.exp(2j * np.pi * ab / 64.0) / 64.0
    w64rh, w64rl = _hilo_const(w64.real.astype(f32))
    w64ih, w64il = _hilo_const(w64.imag.astype(f32))
    w64nh, w64nl = _hilo_const((-w64.imag).astype(f32))
    for nm, m in (("w64rh", w64rh), ("w64rl", w64rl), ("w64ih", w64ih),
                  ("w64il", w64il), ("w64nh", w64nh), ("w64nl", w64nl)):
        base[nm] = _pad128(m, BF)
    tw = np.exp(2j * np.pi * ab / 4096.0)
    base["twr"] = np.ascontiguousarray(tw.real.astype(f32))
    base["twi"] = np.ascontiguousarray(tw.imag.astype(f32))
    t = np.arange(4096, dtype=f32) / f32(4096)
    om = np.exp(np.complex64(-2j * np.pi) * t).astype(np.complex64)
    w_ = ((np.complex64(1) - om) / (np.complex64(1) + om)).astype(np.complex64)
    c_ = (np.complex64(2) / (np.complex64(1) + om)).astype(np.complex64)
    base["wre"] = np.ascontiguousarray(w_.real.reshape(64, 64))
    base["wim"] = np.ascontiguousarray(w_.imag.reshape(64, 64))
    base["ccre"] = np.ascontiguousarray(c_.real.reshape(64, 64))
    base["ccim"] = np.ascontiguousarray(c_.imag.reshape(64, 64))

    ins = []
    for c in range(NCORE):
        d = dict(base)
        d["xo"] = np.ascontiguousarray(x[c * TPC:(c + 1) * TPC])
        ins.append(d)
    return ins



def build():
    nc = bass.Bass("TRN2", target_bir_lowering=False)
    dt = nc.dram_tensor

    def inp(name, shape, dtype=F32):
        return dt(name, list(shape), dtype, kind="ExternalInput").ap()

    xo = inp("xo", [TPC, D])
    xf = inp("xf", [S, D])
    xT = inp("xT", [D, S])
    wq = inp("wq", [D, HD]); wk = inp("wk", [D, HD]); wv = inp("wv", [D, HD])
    bq_cols = inp("bq_cols", [128, 8]); bv_cols = inp("bv_cols", [128, 8])
    bo_row = inp("bo_row", [1, D]); wo = inp("wo", [HD, D])
    rwT = inp("rwT", [D, 8]); aT = inp("aT", [D, 128]); bstk = inp("bstk", [128, D])
    gate_w = inp("gate_w", [D, 8]); up_w = inp("up_w", [D, 2 * D])
    down_w = inp("down_w", [2 * D, D]); fc1_w = inp("fc1_w", [D, 2048])
    fc2_w = inp("fc2_w", [2048, D]); exp_w = inp("exp_w", [2048, 2048])
    gam_row = inp("gam_row", [1, D]); bet_row = inp("bet_row", [1, D])
    lr_c = inp("lr_c", [64, 1]); li_c = inp("li_c", [64, 1])
    pr_c = inp("pr_c", [64, 1]); pi_c = inp("pi_c", [64, 1])
    qr_c = inp("qr_c", [64, 1]); qi_c = inp("qi_c", [64, 1])
    ctr_c = inp("ctr_c", [64, 1]); cti_c = inp("cti_c", [64, 1])
    vcrT = inp("vcrT", [64, 64]); vciT = inp("vciT", [64, 64])
    bp_c = inp("bp_c", [64, 1]); dd = inp("dd", [1, 1]); lstep = inp("lstep", [1, 1])
    identf = inp("identf", [128, 128]); identb = inp("identb", [128, 128], BF16)
    jrev = inp("jrev", [128, 128], BF16)
    in64 = inp("in64", [128, 4096], BF16); e1c = inp("e1c", [128, 4096], BF16)
    w64c = {nm: inp(nm, [128, 64], BF16)
            for nm in ("w64rh", "w64rl", "w64ih", "w64il", "w64nh", "w64nl")}
    twr = inp("twr", [64, 64]); twi = inp("twi", [64, 64])
    wre = inp("wre", [64, 64]); wim = inp("wim", [64, 64])
    ccre = inp("ccre", [64, 64]); ccim = inp("ccim", [64, 64])

    y_t = dt("y", [TPC, D], F32, kind="ExternalOutput").ap()

    kt_d = dt("kt_d", [8, 128, S], BF16, kind="Internal").ap()
    v_d = dt("v_d", [S, HD], BF16, kind="Internal").ap()
    x1_d = dt("x1_d", [TPC, D], F32, kind="Internal").ap()
    oa_d = dt("oa_d", [TPC, D], F32, kind="Internal").ap()
    ac_d = dt("ac_d", [TPC, D], F32, kind="Internal").ap()
    ht_d = dt("ht_d", [64, 128, TPC], BF16, kind="Internal").ap()
    moe_d = dt("moe_d", [2048, TPC], BF16, kind="Internal").ap()
    rows_d = dt("rows_d", [4, S], BF16, kind="Internal").ap()
    zp_d = dt("zp_d", [5120], BF16, kind="Internal").ap()

    taps = {}

    def tapdef(name, shape, dtype=F32):
        if name in TAPS:
            taps[name] = dt("tap_" + name, list(shape), dtype,
                            kind="ExternalOutput").ap()

    tapdef("mcols", [128, 64]); tapdef("rcols", [128, 64])
    tapdef("mocols", [128, 8]); tapdef("rocols", [128, 8])
    tapdef("qt", [128, 8, TPC], BF16); tapdef("wht", [128, 8, 128], BF16)
    tapdef("scl", [128, 8, 8])
    tapdef("kt", [8, 128, S], BF16); tapdef("vv", [S, HD], BF16)
    tapdef("x1", [TPC, D]); tapdef("oa", [TPC, D]); tapdef("ac", [TPC, D])
    tapdef("kb", [64, 64], BF16)
    tapdef("m2", [128, 8]); tapdef("r2", [128, 8]); tapdef("sw", [128, 8])
    tapdef("ht", [64, 128, TPC], BF16); tapdef("moe", [2048, TPC], BF16)

    cl = {"dS": 0, "dG": 0, "eV": 0, "eS": 0, "eT": 0}
    rel = {}

    with contextlib.ExitStack() as root:
        er = root.enter_context
        dS = er(nc.semaphore("dS")); dG = er(nc.semaphore("dG"))
        eV = er(nc.semaphore("eV")); eS = er(nc.semaphore("eS"))
        eT = er(nc.semaphore("eT"))
        SEMO = {"dS": dS, "dG": dG, "eV": eV, "eS": eS, "eT": eT}

        def dinc(i):
            cl["dS"] += 16; i.then_inc(dS, 16); return cl["dS"]

        def ginc(i):
            cl["dG"] += 16; i.then_inc(dG, 16); return cl["dG"]

        def vinc(i):
            cl["eV"] += 1; i.then_inc(eV, 1); return cl["eV"]

        def sinc(i):
            cl["eS"] += 1; i.then_inc(eS, 1); return cl["eS"]

        def tinc(i):
            cl["eT"] += 1; i.then_inc(eT, 1); return cl["eT"]

        def VS(v, i):
            c = vinc(i); v.wait_ge(eV, c); return c

        def awaitr(eng, key):
            r = rel.get(key)
            if r:
                eng.wait_ge(SEMO[r[0]], r[1])

        def setrel(key, sem, cnt):
            rel[key] = (sem, cnt)

        def hilo(v, src, hi, lo, tmpf):
            VS(v, v.tensor_copy(hi, src))
            VS(v, v.tensor_copy(tmpf, hi))
            VS(v, v.tensor_tensor(lo, src, tmpf, ALU.subtract))

        pA = er(nc.psum_tensor("pA", [128, 512], F32))
        pB = er(nc.psum_tensor("pB", [128, 512], F32))
        pC = er(nc.psum_tensor("pC", [128, 512], F32))
        pD = er(nc.psum_tensor("pD", [128, 512], F32))
        pE = er(nc.psum_tensor("pE", [128, 512], F32))
        pO = er(nc.psum_tensor("pO", [128, 1024], F32))
        pbf = er(nc.psum_tensor("pbf", [128, 1024], BF16))
        banks = [pA, pB, pC, pD]
        bnames = ["pA", "pB", "pC", "pD"]

        ones128b = er(nc.sbuf_tensor("ones128b", [128, 128], BF16))
        onesf = er(nc.sbuf_tensor("onesf", [128, 128], F32))
        idf = er(nc.sbuf_tensor("idf", [128, 128], F32))
        idb = er(nc.sbuf_tensor("idb", [128, 128], BF16))
        jbuf = er(nc.sbuf_tensor("jbuf", [128, 128], BF16))
        c32c = er(nc.sbuf_tensor("c32c", [128, 1], F32))
        mcols = er(nc.sbuf_tensor("mcols", [128, 64], F32))
        rcols = er(nc.sbuf_tensor("rcols", [128, 64], F32))
        mocols = er(nc.sbuf_tensor("mocols", [128, 8], F32))
        rocols = er(nc.sbuf_tensor("rocols", [128, 8], F32))
        m2cols = er(nc.sbuf_tensor("m2cols", [128, 8], F32))
        r2cols = er(nc.sbuf_tensor("r2cols", [128, 8], F32))
        a1cols = er(nc.sbuf_tensor("a1cols", [128, 8], F32))
        bncols = er(nc.sbuf_tensor("bncols", [128, 8], F32))
        swcols = er(nc.sbuf_tensor("swcols", [128, 8], F32))
        sclb = er(nc.sbuf_tensor("sclb", [128, 8, 8], F32))
        bqco = er(nc.sbuf_tensor("bqco", [128, 8], F32))
        bvco = er(nc.sbuf_tensor("bvco", [128, 8], F32))
        bqs = er(nc.sbuf_tensor("bqs", [128, 8], F32))
        s1c = er(nc.sbuf_tensor("s1c", [128, 1], F32))
        s2c = er(nc.sbuf_tensor("s2c", [128, 1], F32))
        v1c = er(nc.sbuf_tensor("v1c", [128, 1], F32))
        v2c = er(nc.sbuf_tensor("v2c", [128, 1], F32))
        v3c = er(nc.sbuf_tensor("v3c", [128, 1], F32))
        v4c = er(nc.sbuf_tensor("v4c", [128, 1], F32))
        dcolf = er(nc.sbuf_tensor("dcolf", [128, 1], F32))
        lgsb = er(nc.sbuf_tensor("lgsb", [128, 8], F32))
        top8 = er(nc.sbuf_tensor("top8", [128, 8], F32))
        thrn = er(nc.sbuf_tensor("thrn", [128, 1], F32))
        sgnb = er(nc.sbuf_tensor("sgnb", [128, 8], F32))
        mskb = er(nc.sbuf_tensor("mskb", [128, 8], F32))
        QTb = er(nc.sbuf_tensor("QTb", [128, 8, TPC], BF16))
        whT = er(nc.sbuf_tensor("whT", [128, 8, 128], BF16))

        block = er(nc.Block())

        def barrier():
            snap = dict(cl)

            @block.sync
            def _(q):
                for s_ in ("dS", "dG", "eV", "eS", "eT"):
                    q.wait_ge(SEMO[s_], snap[s_])

            @block.gpsimd
            def _(q):
                for s_ in ("dS", "dG", "eV", "eS", "eT"):
                    q.wait_ge(SEMO[s_], snap[s_])

            @block.vector
            def _(v):
                for s_ in ("dS", "dG", "eV", "eS", "eT"):
                    v.wait_ge(SEMO[s_], snap[s_])

            @block.scalar
            def _(sc):
                for s_ in ("dS", "dG", "eV", "eS", "eT"):
                    sc.wait_ge(SEMO[s_], snap[s_])

            @block.tensor
            def _(t):
                for s_ in ("dS", "dG", "eV", "eS", "eT"):
                    t.wait_ge(SEMO[s_], snap[s_])

        def stats_tile(src_ap, slot, xbuf, sqb, mdst, rdst, keep_x):
            @block.sync
            def _(q):
                awaitr(q, f"xb{slot}")
                cnt = dinc(q.dma_start(out=xbuf[:, slot, :], in_=src_ap))
                rel[f"xdma{slot}"] = ("dS", cnt)

            @block.scalar
            def _(sc):
                sc.wait_ge(dS, rel[f"xdma{slot}"][1])
                awaitr(sc, f"sq{slot}")
                c = sinc(sc.activation(sqb[:, slot, :], xbuf[:, slot, :],
                                       ACT.Square))
                rel[f"sqw{slot}"] = ("eS", c)

            @block.vector
            def _(v):
                v.wait_ge(dS, rel[f"xdma{slot}"][1])
                i1 = vinc(v.reduce_sum(s1c[:], xbuf[:, slot, :], axis=AX))
                v.wait_ge(eS, rel[f"sqw{slot}"][1])
                i2 = vinc(v.reduce_sum(s2c[:], sqb[:, slot, :], axis=AX))
                setrel(f"sq{slot}", "eV", i2)
                if not keep_x:
                    setrel(f"xb{slot}", "eV", i2)
                v.wait_ge(eV, i2)
                VS(v, v.tensor_scalar_mul(mdst, s1c[:], R4))
                VS(v, v.tensor_tensor(v1c[:], mdst, mdst, ALU.mult))
                VS(v, v.tensor_scalar_mul(v2c[:], s2c[:], R4))
                VS(v, v.tensor_tensor(v3c[:], v2c[:], v1c[:], ALU.subtract))
                VS(v, v.tensor_scalar_add(v3c[:], v3c[:], EPS))

            @block.scalar
            def _(sc):
                sc.wait_ge(eV, cl["eV"])
                sinc(sc.activation(v1c[:], v3c[:], ACT.Sqrt))

            @block.vector
            def _(v):
                v.wait_ge(eS, cl["eS"])
                c = vinc(v.reciprocal(rdst, v1c[:]))
                v.wait_ge(eV, c)

        @block.sync
        def _(q):
            dinc(q.dma_start(out=idf[:], in_=identf))
            dinc(q.dma_start(out=bqco[:], in_=bq_cols))
            dinc(q.dma_start(out=bvco[:], in_=bv_cols))

        @block.gpsimd
        def _(q):
            ginc(q.dma_start(out=idb[:], in_=identb))
            ginc(q.dma_start(out=jbuf[:], in_=jrev))

        @block.vector
        def _(v):
            v.memset(ones128b[:], 1.0)
            v.memset(onesf[:], 1.0)
            v.memset(c32c[:], 0.03125)
            v.wait_ge(dS, cl["dS"])
            vinc(v.tensor_scalar_mul(bqs[:], bqco[:], 0.03125))

        with contextlib.ExitStack() as sa:
            ea = sa.enter_context
            xbufA = ea(nc.sbuf_tensor("xbufA", [128, 2, D], F32))
            sqbA = ea(nc.sbuf_tensor("sqbA", [128, 2, D], F32))
            xhib = ea(nc.sbuf_tensor("xhibA", [128, D], BF16))
            xlob = ea(nc.sbuf_tensor("xlobA", [128, D], BF16))
            xn1T = ea(nc.sbuf_tensor("xn1T", [128, 32, TPC], BF16))
            xloT = ea(nc.sbuf_tensor("xloT", [128, 32, 128], BF16))
            wqt = ea(nc.sbuf_tensor("wqt", [128, 2, 32, 128], BF16))
            rwf = ea(nc.sbuf_tensor("rwf", [128, 32, 8], F32))
            rwtm = ea(nc.sbuf_tensor("rwtm", [128, 32, 8], F32))
            rwhi = ea(nc.sbuf_tensor("rwhi", [128, 32, 8], BF16))
            rwlo = ea(nc.sbuf_tensor("rwlo", [128, 32, 8], BF16))
            atb = ea(nc.sbuf_tensor("atb", [128, 32, 128], BF16))
            rowev = ea(nc.sbuf_tensor("rowev", [64, 4, 128], BF16))
            coltA = ea(nc.sbuf_tensor("coltA", [128, 64], F32))
            chbA = ea(nc.sbuf_tensor("chbA", [128, 64], BF16))
            clbA = ea(nc.sbuf_tensor("clbA", [128, 64], BF16))
            whb = ea(nc.sbuf_tensor("whbA", [128, 128], BF16))

            @block.gpsimd
            def _(q):
                ginc(q.dma_start(out=atb[:], in_=bass.AP(
                    aT.tensor, 0, [[128, 128], [128 * 128, 32], [1, 128]])))

            @block.sync
            def _(q):
                dinc(q.dma_start(out=rwf[:], in_=bass.AP(
                    rwT.tensor, 0, [[8, 128], [1024, 32], [1, 8]]),
                    allow_slow_non_contiguous=True))

            @block.vector
            def _(v):
                v.wait_ge(dS, cl["dS"])
                hilo(v, rwf[:], rwhi[:], rwlo[:], rwtm[:])

            for ti in range(64):
                stats_tile(bass.AP(xf.tensor, ti * 128 * D, [[D, 128], [1, D]]),
                           ti % 2, xbufA, sqbA,
                           mcols[:, ti:ti + 1], rcols[:, ti:ti + 1], False)

            if "mcols" in TAPS:
                @block.sync
                def _(q):
                    q.wait_ge(eV, cl["eV"])
                    dinc(q.dma_start(out=taps["mcols"], in_=mcols[:]))
                    dinc(q.dma_start(out=taps["rcols"], in_=rcols[:]))

            @block.vector
            def _(v):
                v.wait_ge(eV, cl["eV"])
                v.wait_ge(eS, cl["eS"])
                hilo(v, mcols[:], chbA[:], clbA[:], coltA[:])

            @block.tensor
            def _(t):
                t.wait_ge(eV, cl["eV"])
                t.transpose(pbf[0:64, 0:128], chbA[:], idb[:])
                tinc(t.transpose(pbf[0:64, 128:256], clbA[:], idb[:]))

            @block.vector
            def _(v):
                v.wait_ge(eT, cl["eT"])
                v.tensor_copy(rowev[:, 0, :], pbf[0:64, 0:128])
                c = vinc(v.tensor_copy(rowev[:, 1, :], pbf[0:64, 128:256]))
                v.wait_ge(eV, c)
                hilo(v, rcols[:], chbA[:], clbA[:], coltA[:])

            @block.tensor
            def _(t):
                t.wait_ge(eV, cl["eV"])
                t.transpose(pbf[0:64, 256:384], chbA[:], idb[:])
                tinc(t.transpose(pbf[0:64, 384:512], clbA[:], idb[:]))

            @block.vector
            def _(v):
                v.wait_ge(eT, cl["eT"])
                v.tensor_copy(rowev[:, 2, :], pbf[0:64, 256:384])
                vinc(v.tensor_copy(rowev[:, 3, :], pbf[0:64, 384:512]))

            @block.sync
            def _(q):
                q.wait_ge(eV, cl["eV"])
                for k in range(4):
                    dinc(q.dma_start(
                        out=bass.AP(rows_d.tensor, k * S, [[128, 64], [1, 128]]),
                        in_=rowev[:, k, :]))

            for ti in range(8):
                slot = ti % 2
                stats_tile(bass.AP(xo.tensor, ti * 128 * D, [[D, 128], [1, D]]),
                           slot, xbufA, sqbA,
                           mocols[:, ti:ti + 1], rocols[:, ti:ti + 1], True)

                @block.vector
                def _(v, ti=ti, slot=slot):
                    awaitr(v, "xhl")
                    VS(v, v.tensor_scalar(sqbA[:, slot, :], xbufA[:, slot, :],
                                          mocols[:, ti:ti + 1],
                                          rocols[:, ti:ti + 1],
                                          ALU.subtract, ALU.mult))
                    VS(v, v.tensor_copy(xhib[:], sqbA[:, slot, :]))
                    VS(v, v.tensor_copy(xbufA[:, slot, :], xhib[:]))
                    VS(v, v.tensor_tensor(xlob[:], sqbA[:, slot, :],
                                          xbufA[:, slot, :], ALU.subtract))
                    setrel(f"xb{slot}", "eV", cl["eV"])
                    setrel(f"sq{slot}", "eV", cl["eV"])

                for part, src, dstT in (("hi", xhib, None), ("lo", xlob, None)):
                    for grp in range(8):
                        @block.tensor
                        def _(t, ti=ti, grp=grp, src=src):
                            if grp == 0:
                                t.wait_ge(eV, cl["eV"])
                            awaitr(t, "pbfq")
                            for j in range(4):
                                a = grp * 4 + j
                                ii = t.transpose(pbf[:, j * 128:(j + 1) * 128],
                                                 src[:, a * 128:(a + 1) * 128],
                                                 idb[:])
                            tinc(ii)

                        @block.vector
                        def _(v, ti=ti, grp=grp, part=part):
                            v.wait_ge(eT, cl["eT"])
                            for j in range(4):
                                a = grp * 4 + j
                                if part == "hi":
                                    dst = xn1T[:, a, ti * 128:(ti + 1) * 128]
                                else:
                                    dst = xloT[:, a, :]
                                ii = v.tensor_copy(dst,
                                                   pbf[:, j * 128:(j + 1) * 128])
                            setrel("pbfq", "eV", vinc(ii))

                setrel("xhl", "eT", cl["eT"])

                @block.tensor
                def _(t, ti=ti):
                    t.wait_ge(eV, cl["eV"])
                    awaitr(t, "pE")
                    first = True
                    for a in range(32):
                        t.matmul(pE[:, 0:8],
                                 lhsT=xn1T[:, a, ti * 128:(ti + 1) * 128],
                                 rhs=rwhi[:, a, :], start=first, stop=False)
                        first = False
                    for a in range(32):
                        t.matmul(pE[:, 0:8], lhsT=xloT[:, a, :],
                                 rhs=rwhi[:, a, :], start=False, stop=False)
                    for a in range(32):
                        ii = t.matmul(pE[:, 0:8],
                                      lhsT=xn1T[:, a, ti * 128:(ti + 1) * 128],
                                      rhs=rwlo[:, a, :], start=False,
                                      stop=(a == 31))
                    tinc(ii)

                @block.vector
                def _(v, ti=ti):
                    v.wait_ge(eT, cl["eT"])
                    c = vinc(v.tensor_copy(lgsb[:], pE[:, 0:8]))
                    setrel("pE", "eV", c)
                    v.wait_ge(eV, c)
                    VS(v, v.max(top8[:], lgsb[:]))
                    VS(v, v.tensor_add(thrn[:], top8[:, 0:1], top8[:, 1:2]))
                    VS(v, v.tensor_scalar_mul(thrn[:], thrn[:], -0.5))

                @block.scalar
                def _(sc, ti=ti):
                    sc.wait_ge(eV, cl["eV"])
                    sinc(sc.activation(sgnb[:], lgsb[:], ACT.Sign, bias=thrn[:]))

                @block.vector
                def _(v, ti=ti):
                    v.wait_ge(eS, cl["eS"])
                    VS(v, v.tensor_scalar(mskb[:], sgnb[:], 0.5, 0.5,
                                          ALU.mult, ALU.add))
                    VS(v, v.tensor_scalar(sclb[:, ti, :], mskb[:],
                                          top8[:, 0:1], None, ALU.mult))

            if "mocols" in TAPS:
                @block.sync
                def _(q):
                    q.wait_ge(eV, cl["eV"])
                    dinc(q.dma_start(out=taps["mocols"], in_=mocols[:]))
                    dinc(q.dma_start(out=taps["rocols"], in_=rocols[:]))
            if "scl" in TAPS:
                @block.sync
                def _(q):
                    q.wait_ge(eV, cl["eV"])
                    dinc(q.dma_start(out=taps["scl"], in_=sclb[:]))

            for ft in range(8):
                slot = ft % 2

                @block.gpsimd
                def _(q, ft=ft, slot=slot):
                    awaitr(q, f"wqt{slot}")
                    c = ginc(q.dma_start(out=wqt[:, slot, :, :], in_=bass.AP(
                        wq.tensor, ft * 128,
                        [[HD, 128], [128 * HD, 32], [1, 128]])))
                    rel[f"wqd{slot}"] = ("dG", c)

                for nch in range(2):
                    bk = banks[(ft * 2 + nch) % 2]
                    bn = bnames[(ft * 2 + nch) % 2]

                    @block.tensor
                    def _(t, ft=ft, nch=nch, slot=slot, bk=bk, bn=bn):
                        t.wait_ge(dG, rel[f"wqd{slot}"][1])
                        awaitr(t, bn)
                        for a in range(32):
                            ii = t.matmul(bk[:], lhsT=wqt[:, slot, a, :],
                                          rhs=xn1T[:, a, nch * 512:(nch + 1) * 512],
                                          start=(a == 0), stop=(a == 31))
                        c = tinc(ii)
                        if nch == 1:
                            setrel(f"wqt{slot}", "eT", c)

                    @block.vector
                    def _(v, ft=ft, nch=nch, bk=bk, bn=bn):
                        v.wait_ge(eT, cl["eT"])
                        c = vinc(v.tensor_scalar(
                            QTb[:, ft, nch * 512:(nch + 1) * 512], bk[:],
                            c32c[:], bqs[:, ft:ft + 1], ALU.mult, ALU.add))
                        setrel(bn, "eV", c)

            for ti in range(8):
                @block.tensor
                def _(t, ti=ti):
                    awaitr(t, "pCh")
                    first = True
                    for a in range(32):
                        ii = t.matmul(pC[:, 0:128],
                                      lhsT=xn1T[:, a, ti * 128:(ti + 1) * 128],
                                      rhs=atb[:, a, :], start=first,
                                      stop=(a == 31))
                        first = False
                    tinc(ii)

                @block.vector
                def _(v, ti=ti):
                    v.wait_ge(eT, cl["eT"])
                    for e in range(8):
                        ii = v.tensor_scalar(whb[:, e * 16:(e + 1) * 16],
                                             pC[:, e * 16:(e + 1) * 16],
                                             sclb[:, ti, e:e + 1], None,
                                             ALU.mult)
                    c = vinc(ii)
                    setrel("pCh", "eV", c)

                @block.tensor
                def _(t, ti=ti):
                    t.wait_ge(eV, cl["eV"])
                    awaitr(t, "pbfq")
                    tinc(t.transpose(pbf[:, 0:128], whb[:], idb[:]))

                @block.vector
                def _(v, ti=ti):
                    v.wait_ge(eT, cl["eT"])
                    c = vinc(v.tensor_copy(whT[:, ti, :], pbf[:, 0:128]))
                    setrel("pbfq", "eV", c)

            if "qt" in TAPS:
                @block.sync
                def _(q):
                    q.wait_ge(eV, cl["eV"])
                    dinc(q.dma_start(out=taps["qt"], in_=QTb[:]))
            if "wht" in TAPS:
                @block.sync
                def _(q):
                    q.wait_ge(eV, cl["eV"])
                    dinc(q.dma_start(out=taps["wht"], in_=whT[:]))

            barrier()
            rel.clear()

        if _ph("B"):
            with contextlib.ExitStack() as sb:
                eb = sb.enter_context
                wkvb = eb(nc.sbuf_tensor("wkvb", [128, 32, HD], BF16))
                xTb = eb(nc.sbuf_tensor("xTb", [128, 2, 32, 512], BF16))
                ktbf = eb(nc.sbuf_tensor("ktbf", [128, 2, 8, 512], BF16))
                vbf = eb(nc.sbuf_tensor("vbf", [128, 2, 4, HD], BF16))
                rrepS = eb(nc.sbuf_tensor("rrepS", [128, 2, 512], F32))
                mpd = eb(nc.sbuf_tensor("mpd", [128, 1024], BF16))
                rpd = eb(nc.sbuf_tensor("rpd", [128, 1024], BF16))
                mtp = eb(nc.sbuf_tensor("mtp", [128, 1024], BF16))
                cspad = eb(nc.sbuf_tensor("cspad", [128, 1024], BF16))
                csrow = eb(nc.sbuf_tensor("csrow", [1, HD], F32))
                cstm = eb(nc.sbuf_tensor("cstm", [1, HD], F32))
                cshb = eb(nc.sbuf_tensor("cshb", [1, HD], BF16))
                cslb = eb(nc.sbuf_tensor("cslb", [1, HD], BF16))

                @block.vector
                def _(v):
                    v.memset(mpd[:], 0.0)
                    v.memset(rpd[:], 0.0)
                    v.memset(mtp[:], 0.0)
                    vinc(v.memset(cspad[:], 0.0))

                def cs_build(pattern):
                    @block.tensor
                    def _(t):
                        t.wait_ge(dG, cl["dG"])
                        awaitr(t, "pA"); awaitr(t, "pB")
                        for a in range(32):
                            t.matmul(pA[:], lhsT=ones128b[:],
                                     rhs=wkvb[:, a, 0:512],
                                     start=(a == 0), stop=(a == 31))
                        for a in range(32):
                            ii = t.matmul(pB[:], lhsT=ones128b[:],
                                          rhs=wkvb[:, a, 512:1024],
                                          start=(a == 0), stop=(a == 31))
                        tinc(ii)

                    @block.vector
                    def _(v):
                        v.wait_ge(eT, cl["eT"])
                        v.tensor_copy(csrow[0:1, 0:512], pA[0:1, :])
                        c = vinc(v.tensor_copy(csrow[0:1, 512:1024], pB[0:1, :]))
                        setrel("pA", "eV", c); setrel("pB", "eV", c)
                        v.wait_ge(eV, c)
                        VS(v, v.tensor_scalar_mul(csrow[:], csrow[:], -1.0))
                        hilo(v, csrow[:], cshb[:], cslb[:], cstm[:])
                        srcs = {"h": cshb, "l": cslb}
                        v.tensor_copy(cspad[0:1, :], srcs[pattern[0]][:])
                        v.tensor_copy(cspad[32:33, :], srcs[pattern[1]][:])
                        c = vinc(v.tensor_copy(cspad[64:65, :],
                                               srcs[pattern[2]][:]))
                        v.wait_ge(eV, c)

                @block.gpsimd
                def _(q):
                    ginc(q.dma_start(out=wkvb[:], in_=bass.AP(
                        wk.tensor, 0, [[HD, 128], [128 * HD, 32], [1, HD]])))

                cs_build("hhl")

                for c in range(16):
                    slot = c % 2

                    @block.gpsimd
                    def _(q, c=c, slot=slot):
                        awaitr(q, f"xtb{slot}")
                        cc = ginc(q.dma_start(out=xTb[:, slot, :, :],
                                              in_=bass.AP(xT.tensor, c * 512,
                                                          [[S, 128],
                                                           [128 * S, 32],
                                                           [1, 512]])))
                        rel[f"xtd{slot}"] = ("dG", cc)

                    @block.sync
                    def _(q, c=c, slot=slot):
                        awaitr(q, f"mpd{slot}")
                        h = slot * 512
                        dinc(q.dma_start(out=mpd[0:1, h:h + 512], in_=bass.AP(
                            rows_d.tensor, 0 * S + c * 512, [[1, 1], [1, 512]])))
                        dinc(q.dma_start(out=mpd[32:33, h:h + 512], in_=bass.AP(
                            rows_d.tensor, 1 * S + c * 512, [[1, 1], [1, 512]])))
                        dinc(q.dma_start(out=mpd[64:65, h:h + 512], in_=bass.AP(
                            rows_d.tensor, 0 * S + c * 512, [[1, 1], [1, 512]])))
                        dinc(q.dma_start(out=rpd[0:1, h:h + 512], in_=bass.AP(
                            rows_d.tensor, 2 * S + c * 512, [[1, 1], [1, 512]])))
                        cc = dinc(q.dma_start(out=rpd[32:33, h:h + 512],
                                              in_=bass.AP(rows_d.tensor,
                                                          3 * S + c * 512,
                                                          [[1, 1], [1, 512]])))
                        rel[f"mpdd{slot}"] = ("dS", cc)

                    @block.tensor
                    def _(t, c=c, slot=slot):
                        t.wait_ge(dG, rel[f"xtd{slot}"][1])
                        t.wait_ge(dS, rel[f"mpdd{slot}"][1])
                        h = slot * 512
                        awaitr(t, "pE")
                        cc = tinc(t.matmul(pE[:], lhsT=ones128b[:],
                                           rhs=rpd[:, h:h + 512],
                                           start=True, stop=True))
                        rel["rrepmm"] = ("eT", cc)
                        for ft in range(8):
                            bk = banks[ft % 3]
                            awaitr(t, bnames[ft % 3])
                            for a in range(32):
                                t.matmul(bk[:],
                                         lhsT=wkvb[:, a, ft * 128:(ft + 1) * 128],
                                         rhs=xTb[:, slot, a, :],
                                         start=(a == 0), stop=False)
                            cc = tinc(t.matmul(
                                bk[:], lhsT=cspad[:, ft * 128:(ft + 1) * 128],
                                rhs=mpd[:, h:h + 512], start=False, stop=True))
                            rel[f"kmm{ft}"] = ("eT", cc)
                            if ft == 7:
                                setrel(f"xtb{slot}", "eT", cc)
                                setrel(f"mpd{slot}", "eT", cc)

                    @block.vector
                    def _(v, c=c, slot=slot):
                        v.wait_ge(eT, rel["rrepmm"][1])
                        awaitr(v, f"ktb{slot}")
                        cc = vinc(v.tensor_copy(rrepS[:, slot, :], pE[:]))
                        setrel("pE", "eV", cc)
                        v.wait_ge(eV, cc)
                        for ft in range(8):
                            v.wait_ge(eT, rel[f"kmm{ft}"][1])
                            cc = vinc(v.tensor_tensor(ktbf[:, slot, ft, :],
                                                      banks[ft % 3][:],
                                                      rrepS[:, slot, :],
                                                      ALU.mult))
                            setrel(bnames[ft % 3], "eV", cc)
                        rel[f"ktw{slot}"] = ("eV", cc)

                    @block.sync
                    def _(q, c=c, slot=slot):
                        q.wait_ge(eV, rel[f"ktw{slot}"][1])
                        cc = dinc(q.dma_start(
                            out=bass.AP(kt_d.tensor, c * 512,
                                        [[S, 128], [128 * S, 8], [1, 512]]),
                            in_=ktbf[:, slot, :, :]))
                        setrel(f"ktb{slot}", "dS", cc)

                @block.gpsimd
                def _(q):
                    q.wait_ge(eT, cl["eT"])
                    ginc(q.dma_start(out=wkvb[:], in_=bass.AP(
                        wv.tensor, 0, [[HD, 128], [128 * HD, 32], [1, HD]])))

                cs_build("hlh")

                for c in range(16):
                    slot = c % 2

                    @block.gpsimd
                    def _(q, c=c, slot=slot):
                        awaitr(q, f"xtb{slot}")
                        cc = ginc(q.dma_start(out=xTb[:, slot, :, :],
                                              in_=bass.AP(xT.tensor, c * 512,
                                                          [[S, 128],
                                                           [128 * S, 32],
                                                           [1, 512]])))
                        rel[f"xtd{slot}"] = ("dG", cc)

                    @block.sync
                    def _(q, c=c, slot=slot):
                        awaitr(q, f"mtp{slot}")
                        h = slot * 512
                        for tt_ in range(4):
                            off = c * 512 + tt_ * 128
                            ss = h + tt_ * 128
                            dinc(q.dma_start(out=mtp[0:1, ss:ss + 128],
                                             in_=bass.AP(rows_d.tensor, off,
                                                         [[1, 1], [1, 128]])))
                            dinc(q.dma_start(out=mtp[32:33, ss:ss + 128],
                                             in_=bass.AP(rows_d.tensor, off,
                                                         [[1, 1], [1, 128]])))
                            cc = dinc(q.dma_start(out=mtp[64:65, ss:ss + 128],
                                                  in_=bass.AP(rows_d.tensor,
                                                              S + off,
                                                              [[1, 1],
                                                               [1, 128]])))
                        rel[f"mtpd{slot}"] = ("dS", cc)

                    @block.tensor
                    def _(t, c=c, slot=slot):
                        t.wait_ge(dG, rel[f"xtd{slot}"][1])
                        t.wait_ge(dS, rel[f"mtpd{slot}"][1])
                        h = slot * 512
                        for tt_ in range(4):
                            for nch in range(2):
                                k6 = tt_ * 2 + nch
                                bk = banks[k6 % 3]
                                awaitr(t, bnames[k6 % 3])
                                for a in range(32):
                                    t.matmul(
                                        bk[:],
                                        lhsT=xTb[:, slot, a,
                                                 tt_ * 128:(tt_ + 1) * 128],
                                        rhs=wkvb[:, a,
                                                 nch * 512:(nch + 1) * 512],
                                        start=(a == 0), stop=False)
                                cc = tinc(t.matmul(
                                    bk[:],
                                    lhsT=mtp[:, h + tt_ * 128:h + (tt_ + 1) * 128],
                                    rhs=cspad[:, nch * 512:(nch + 1) * 512],
                                    start=False, stop=True))
                                rel[f"vmm{k6}"] = ("eT", cc)
                        setrel(f"xtb{slot}", "eT", cc)
                        setrel(f"mtp{slot}", "eT", cc)

                    @block.vector
                    def _(v, c=c, slot=slot):
                        awaitr(v, f"vbf{slot}")
                        for tt_ in range(4):
                            for nch in range(2):
                                k6 = tt_ * 2 + nch
                                v.wait_ge(eT, rel[f"vmm{k6}"][1])
                                cc = vinc(v.tensor_scalar(
                                    vbf[:, slot, tt_,
                                        nch * 512:(nch + 1) * 512],
                                    banks[k6 % 3][:],
                                    rcols[:, c * 4 + tt_:c * 4 + tt_ + 1],
                                    None, ALU.mult))
                                setrel(bnames[k6 % 3], "eV", cc)
                        rel[f"vbw{slot}"] = ("eV", cc)

                    @block.sync
                    def _(q, c=c, slot=slot):
                        q.wait_ge(eV, rel[f"vbw{slot}"][1])
                        cc = dinc(q.dma_start(
                            out=bass.AP(v_d.tensor, c * 512 * HD,
                                        [[HD, 128], [128 * HD, 4], [1, HD]]),
                            in_=vbf[:, slot, :, :]))
                        setrel(f"vbf{slot}", "dS", cc)

                if "kt" in TAPS:
                    @block.sync
                    def _(q):
                        q.wait_ge(dS, cl["dS"])
                        dinc(q.dma_start(out=taps["kt"], in_=kt_d))
                if "vv" in TAPS:
                    @block.sync
                    def _(q):
                        q.wait_ge(dS, cl["dS"])
                        dinc(q.dma_start(out=taps["vv"], in_=v_d))

                barrier()
                rel.clear()

        if _ph("C"):
            with contextlib.ExitStack() as sc_:
                ec = sc_.enter_context
                bstkb = ec(nc.sbuf_tensor("bstkb", [128, D], BF16))
                biaspad = ec(nc.sbuf_tensor("biaspad", [128, D], BF16))
                wot = ec(nc.sbuf_tensor("wot", [128, 2, 8, 512], BF16))
                Sb = ec(nc.sbuf_tensor("Sb", [128, S], F32))
                Pb = ec(nc.sbuf_tensor("Pb", [128, S], BF16))
                PTb = ec(nc.sbuf_tensor("PTb", [128, 64, 128], BF16))
                ktb = ec(nc.sbuf_tensor("ktb", [128, 2, 8, 512], BF16))
                vb = ec(nc.sbuf_tensor("vb", [128, 2, 4, HD], BF16))
                Obuf = ec(nc.sbuf_tensor("Obuf", [128, HD], F32))
                OTb = ec(nc.sbuf_tensor("OTb", [128, 8, 128], BF16))
                xob = ec(nc.sbuf_tensor("xob", [128, D], F32))
                x1c = ec(nc.sbuf_tensor("x1c", [128, 2, 512], F32))
                mxc = ec(nc.sbuf_tensor("mxc", [128, 1], F32))
                nmxc = ec(nc.sbuf_tensor("nmxc", [128, 1], F32))
                rsmc = ec(nc.sbuf_tensor("rsmc", [128, 1], F32))
                rrc = ec(nc.sbuf_tensor("rrc", [128, 1], F32))

                with contextlib.ExitStack() as sp:
                    ep = sp.enter_context
                    bvrepb = ep(nc.sbuf_tensor("bvrepb", [128, 8, 128], BF16))
                    bvtmp = ep(nc.sbuf_tensor("bvtmp", [128, 128], F32))
                    bopd = ep(nc.sbuf_tensor("bopd", [128, D], BF16))
                    browf = ep(nc.sbuf_tensor("browf", [1, D], F32))

                    @block.gpsimd
                    def _(q):
                        ginc(q.dma_start(out=bstkb[:], in_=bstk))

                    @block.vector
                    def _(v):
                        v.memset(bopd[:], 0.0)
                        c = vinc(v.memset(biaspad[:], 0.0))
                        v.wait_ge(eV, c)

                    @block.gpsimd
                    def _(q):
                        q.wait_ge(eV, cl["eV"])
                        ginc(q.dma_start(out=bopd[0:1, :], in_=bo_row))

                    @block.vector
                    def _(v):
                        for ft in range(8):
                            c = vinc(v.tensor_scalar(bvtmp[:], onesf[:],
                                                     bvco[:, ft:ft + 1], None,
                                                     ALU.mult))
                            v.wait_ge(eV, c)
                            VS(v, v.tensor_copy(bvrepb[:, ft, :], bvtmp[:]))

                    for och in range(8):
                        slot = och % 2

                        @block.gpsimd
                        def _(q, och=och, slot=slot):
                            awaitr(q, f"wot{slot}")
                            c = ginc(q.dma_start(
                                out=wot[:, slot, :, :],
                                in_=bass.AP(wo.tensor, och * 512,
                                            [[D, 128], [128 * D, 8],
                                             [1, 512]])))
                            rel[f"wod{slot}"] = ("dG", c)

                        @block.tensor
                        def _(t, och=och, slot=slot):
                            t.wait_ge(dG, rel[f"wod{slot}"][1])
                            t.wait_ge(eV, cl["eV"])
                            awaitr(t, "pE")
                            for ft in range(8):
                                t.matmul(pE[:], lhsT=bvrepb[:, ft, :],
                                         rhs=wot[:, slot, ft, :],
                                         start=(ft == 0), stop=False)
                            c = tinc(t.matmul(
                                pE[:], lhsT=ones128b[:],
                                rhs=bopd[:, och * 512:(och + 1) * 512],
                                start=False, stop=True))
                            setrel(f"wot{slot}", "eT", c)

                        @block.vector
                        def _(v, och=och):
                            v.wait_ge(eT, cl["eT"])
                            c = vinc(v.tensor_copy(
                                browf[0:1, och * 512:(och + 1) * 512],
                                pE[0:1, :]))
                            setrel("pE", "eV", c)

                    @block.vector
                    def _(v):
                        v.wait_ge(eV, cl["eV"])
                        VS(v, v.tensor_copy(biaspad[0:1, :], browf[:]))

                    barrier()

                for q_ in range(8):
                    for c in range(16):
                        slot = c % 2

                        @block.sync
                        def _(q, c=c, slot=slot, q_=q_):
                            awaitr(q, f"ktb{slot}")
                            cc = dinc(q.dma_start(
                                out=ktb[:, slot, :, :],
                                in_=bass.AP(kt_d.tensor, c * 512,
                                            [[S, 128], [128 * S, 8],
                                             [1, 512]])))
                            rel[f"ktd{slot}"] = ("dS", cc)
                            if c == 0:
                                awaitr(q, "xob")
                                cc = dinc(q.dma_start(
                                    out=xob[:],
                                    in_=bass.AP(xo.tensor, q_ * 128 * D,
                                                [[D, 128], [1, D]])))
                                rel["xobd"] = ("dS", cc)

                        bk = banks[c % 2]
                        bn = bnames[c % 2]

                        @block.tensor
                        def _(t, c=c, slot=slot, q_=q_, bk=bk, bn=bn):
                            t.wait_ge(dS, rel[f"ktd{slot}"][1])
                            awaitr(t, bn)
                            for ft in range(8):
                                ii = t.matmul(
                                    bk[:],
                                    lhsT=QTb[:, ft, q_ * 128:(q_ + 1) * 128],
                                    rhs=ktb[:, slot, ft, :],
                                    start=(ft == 0), stop=(ft == 7))
                            c2 = tinc(ii)
                            setrel(f"ktb{slot}", "eT", c2)

                        @block.vector
                        def _(v, c=c, bk=bk, bn=bn):
                            if c == 0:
                                awaitr(v, "SbFree")
                            v.wait_ge(eT, cl["eT"])
                            cc = vinc(v.tensor_copy(
                                Sb[:, c * 512:(c + 1) * 512], bk[:]))
                            setrel(bn, "eV", cc)

                    @block.vector
                    def _(v):
                        v.wait_ge(eV, cl["eV"])
                        VS(v, v.reduce_max(mxc[:], Sb[:], axis=AX))
                        VS(v, v.tensor_scalar_mul(nmxc[:], mxc[:], -1.0))

                    @block.scalar
                    def _(sc):
                        sc.wait_ge(eV, cl["eV"])
                        cc = sinc(sc.activation(Pb[:], Sb[:], ACT.Exp,
                                                bias=nmxc[:],
                                                accum_out=rsmc[:]))
                        rel["SbFree"] = ("eS", cc)

                    @block.vector
                    def _(v):
                        v.wait_ge(eS, cl["eS"])
                        c = vinc(v.reciprocal(rrc[:], rsmc[:]))
                        v.wait_ge(eV, c)

                    for grp in range(8):
                        @block.tensor
                        def _(t, grp=grp):
                            awaitr(t, "pbfq")
                            if grp == 0:
                                t.wait_ge(eV, cl["eV"])
                            for j in range(8):
                                kt8 = grp * 8 + j
                                ii = t.transpose(pbf[:, j * 128:(j + 1) * 128],
                                                 Pb[:, kt8 * 128:(kt8 + 1) * 128],
                                                 idb[:])
                            tinc(ii)

                        @block.vector
                        def _(v, grp=grp):
                            v.wait_ge(eT, cl["eT"])
                            for j in range(8):
                                kt8 = grp * 8 + j
                                ii = v.tensor_copy(PTb[:, kt8, :],
                                                   pbf[:, j * 128:(j + 1) * 128])
                            setrel("pbfq", "eV", vinc(ii))

                    for c in range(16):
                        slot = c % 2

                        @block.sync
                        def _(q, c=c, slot=slot):
                            awaitr(q, f"vb{slot}")
                            cc = dinc(q.dma_start(
                                out=vb[:, slot, :, :],
                                in_=bass.AP(v_d.tensor, c * 512 * HD,
                                            [[HD, 128], [128 * HD, 4],
                                             [1, HD]])))
                            rel[f"vbd{slot}"] = ("dS", cc)

                        @block.tensor
                        def _(t, c=c, slot=slot):
                            t.wait_ge(dS, rel[f"vbd{slot}"][1])
                            if c == 0:
                                awaitr(t, "pO")
                                t.wait_ge(eV, cl["eV"])
                            for j in range(4):
                                for hf in range(2):
                                    ii = t.matmul(
                                        pO[:, hf * 512:(hf + 1) * 512],
                                        lhsT=PTb[:, c * 4 + j, :],
                                        rhs=vb[:, slot, j,
                                               hf * 512:(hf + 1) * 512],
                                        start=(c == 0 and j == 0),
                                        stop=(c == 15 and j == 3))
                            cc = tinc(ii)
                            setrel(f"vb{slot}", "eT", cc)

                    @block.vector
                    def _(v):
                        v.wait_ge(eT, cl["eT"])
                        v.tensor_scalar(Obuf[:, 0:512], pO[:, 0:512], rrc[:],
                                        None, ALU.mult)
                        c = vinc(v.tensor_scalar(Obuf[:, 512:1024],
                                                 pO[:, 512:1024], rrc[:],
                                                 None, ALU.mult))
                        setrel("pO", "eV", c)
                        v.wait_ge(eV, c)

                    for hf in range(2):
                        bk = banks[hf]
                        bn = bnames[hf]

                        @block.tensor
                        def _(t, hf=hf, bk=bk, bn=bn):
                            awaitr(t, bn)
                            if hf == 0:
                                t.wait_ge(eV, cl["eV"])
                            for j in range(4):
                                ft = hf * 4 + j
                                ii = t.transpose(bk[:, j * 128:(j + 1) * 128],
                                                 Obuf[:, ft * 128:(ft + 1) * 128],
                                                 idf[:])
                            tinc(ii)

                        @block.vector
                        def _(v, hf=hf, bk=bk, bn=bn):
                            v.wait_ge(eT, cl["eT"])
                            for j in range(4):
                                ft = hf * 4 + j
                                ii = v.tensor_copy(OTb[:, ft, :],
                                                   bk[:, j * 128:(j + 1) * 128])
                            setrel(bn, "eV", vinc(ii))

                    for och in range(8):
                        slot = och % 2
                        bk = banks[2 + och % 2]
                        bn = bnames[2 + och % 2]

                        @block.gpsimd
                        def _(q, och=och, slot=slot):
                            awaitr(q, f"wot{slot}")
                            c = ginc(q.dma_start(
                                out=wot[:, slot, :, :],
                                in_=bass.AP(wo.tensor, och * 512,
                                            [[D, 128], [128 * D, 8],
                                             [1, 512]])))
                            rel[f"wod{slot}"] = ("dG", c)

                        @block.tensor
                        def _(t, och=och, slot=slot, q_=q_, bk=bk, bn=bn):
                            t.wait_ge(dG, rel[f"wod{slot}"][1])
                            if och == 0:
                                t.wait_ge(eV, cl["eV"])
                            awaitr(t, bn)
                            for ft in range(8):
                                t.matmul(bk[:], lhsT=OTb[:, ft, :],
                                         rhs=wot[:, slot, ft, :],
                                         start=(ft == 0), stop=False)
                            t.matmul(bk[:], lhsT=whT[:, q_, :],
                                     rhs=bstkb[:, och * 512:(och + 1) * 512],
                                     start=False, stop=False)
                            c = tinc(t.matmul(
                                bk[:], lhsT=ones128b[:],
                                rhs=biaspad[:, och * 512:(och + 1) * 512],
                                start=False, stop=True))
                            setrel(f"wot{slot}", "eT", c)

                        @block.vector
                        def _(v, och=och, slot=slot, bk=bk, bn=bn):
                            v.wait_ge(eT, cl["eT"])
                            v.wait_ge(dS, rel["xobd"][1])
                            awaitr(v, f"x1c{slot}")
                            c = vinc(v.tensor_tensor(
                                x1c[:, slot, :], bk[:],
                                xob[:, och * 512:(och + 1) * 512], ALU.add))
                            setrel(bn, "eV", c)
                            rel[f"x1w{slot}"] = ("eV", c)

                        @block.sync
                        def _(q, och=och, slot=slot, q_=q_):
                            q.wait_ge(eV, rel[f"x1w{slot}"][1])
                            c = dinc(q.dma_start(
                                out=bass.AP(x1_d.tensor,
                                            q_ * 128 * D + och * 512,
                                            [[D, 128], [1, 512]]),
                                in_=x1c[:, slot, :]))
                            setrel(f"x1c{slot}", "dS", c)

                    setrel("xob", "eV", cl["eV"])

                if "x1" in TAPS:
                    @block.sync
                    def _(q):
                        q.wait_ge(dS, cl["dS"])
                        dinc(q.dma_start(out=taps["x1"], in_=x1_d))

                barrier()
                rel.clear()

        with contextlib.ExitStack() as sd0:
            ed0 = sd0.enter_context
            xn2T = ed0(nc.sbuf_tensor("xn2T", [128, 32, TPC], BF16)) \
                if _ph("D") else None

            if _ph("D"):
                with contextlib.ExitStack() as sde:
                    ede = sde.enter_context
                    xbufD = ede(nc.sbuf_tensor("xbufD", [128, 2, D], F32))
                    oaccb = ede(nc.sbuf_tensor("oaccb", [128, D], F32))
                    xn2hib = ede(nc.sbuf_tensor("xn2hib", [128, D], BF16))
                    tmp512 = ede(nc.sbuf_tensor("tmp512", [128, 2, 512], F32))
                    gwb = ede(nc.sbuf_tensor("gwb", [128, 32, 8], BF16))
                    dpad = ede(nc.sbuf_tensor("dpad", [128, 1], BF16))
                    d1b = ede(nc.sbuf_tensor("d1b", [1, 1], BF16))
                    d1f = ede(nc.sbuf_tensor("d1f", [1, 1], F32))
                    d2f = ede(nc.sbuf_tensor("d2f", [1, 1], F32))

                    with contextlib.ExitStack() as sds:
                        eds = sds.enter_context
                        sqbD = eds(nc.sbuf_tensor("sqbD", [128, 2, D], F32))

                        @block.gpsimd
                        def _(q):
                            ginc(q.dma_start(out=gwb[:], in_=bass.AP(
                                gate_w.tensor, 0, [[8, 128], [1024, 32], [1, 8]]),
                                allow_slow_non_contiguous=True))

                        @block.sync
                        def _(q):
                            dinc(q.dma_start(out=d2f[:], in_=dd))

                        @block.vector
                        def _(v):
                            v.memset(dpad[:], 0.0)
                            v.wait_ge(dS, cl["dS"])
                            hilo(v, d2f[:], dpad[0:1, :], d1b[:], d1f[:])
                            VS(v, v.tensor_copy(dpad[32:33, :], d1b[:]))

                        @block.tensor
                        def _(t):
                            t.wait_ge(eV, cl["eV"])
                            awaitr(t, "pE")
                            tinc(t.matmul(pE[:, 0:1], lhsT=ones128b[:],
                                          rhs=dpad[:], start=True, stop=True))

                        @block.vector
                        def _(v):
                            v.wait_ge(eT, cl["eT"])
                            c = vinc(v.tensor_copy(dcolf[:], pE[:, 0:1]))
                            setrel("pE", "eV", c)
                            v.wait_ge(eV, c)

                        for ti in range(8):
                            slot = ti % 2
                            stats_tile(bass.AP(x1_d.tensor, ti * 128 * D,
                                               [[D, 128], [1, D]]),
                                       slot, xbufD, sqbD,
                                       m2cols[:, ti:ti + 1],
                                       r2cols[:, ti:ti + 1], True)

                            @block.vector
                            def _(v, ti=ti, slot=slot):
                                awaitr(v, "xhl2")
                                VS(v, v.tensor_scalar(xn2hib[:],
                                                      xbufD[:, slot, :],
                                                      m2cols[:, ti:ti + 1],
                                                      r2cols[:, ti:ti + 1],
                                                      ALU.subtract, ALU.mult))
                                setrel(f"xb{slot}", "eV", cl["eV"])
                                VS(v, v.tensor_scalar(v4c[:],
                                                      r2cols[:, ti:ti + 1],
                                                      dcolf[:], None, ALU.mult))
                                VS(v, v.tensor_scalar_add(a1cols[:, ti:ti + 1],
                                                          v4c[:], 1.0))
                                VS(v, v.tensor_scalar(v4c[:], v4c[:],
                                                      m2cols[:, ti:ti + 1],
                                                      None, ALU.mult))
                                VS(v, v.tensor_scalar_mul(bncols[:, ti:ti + 1],
                                                          v4c[:], -1.0))

                            for grp in range(8):
                                @block.tensor
                                def _(t, ti=ti, grp=grp):
                                    if grp == 0:
                                        t.wait_ge(eV, cl["eV"])
                                    awaitr(t, "pbfq")
                                    for j in range(4):
                                        a = grp * 4 + j
                                        ii = t.transpose(
                                            pbf[:, j * 128:(j + 1) * 128],
                                            xn2hib[:, a * 128:(a + 1) * 128],
                                            idb[:])
                                    tinc(ii)

                                @block.vector
                                def _(v, ti=ti, grp=grp):
                                    v.wait_ge(eT, cl["eT"])
                                    for j in range(4):
                                        a = grp * 4 + j
                                        ii = v.tensor_copy(
                                            xn2T[:, a, ti * 128:(ti + 1) * 128],
                                            pbf[:, j * 128:(j + 1) * 128])
                                    setrel("pbfq", "eV", vinc(ii))

                            setrel("xhl2", "eT", cl["eT"])

                            @block.tensor
                            def _(t, ti=ti):
                                t.wait_ge(dG, cl["dG"])
                                t.wait_ge(eV, cl["eV"])
                                awaitr(t, "pE")
                                for a in range(32):
                                    ii = t.matmul(
                                        pE[:, 0:8],
                                        lhsT=xn2T[:, a, ti * 128:(ti + 1) * 128],
                                        rhs=gwb[:, a, :],
                                        start=(a == 0), stop=(a == 31))
                                tinc(ii)

                            @block.vector
                            def _(v, ti=ti):
                                v.wait_ge(eT, cl["eT"])
                                c = vinc(v.tensor_copy(lgsb[:], pE[:, 0:8]))
                                setrel("pE", "eV", c)
                                v.wait_ge(eV, c)
                                VS(v, v.max(top8[:], lgsb[:]))
                                VS(v, v.tensor_add(swcols[:, ti:ti + 1],
                                                   top8[:, 0:1], top8[:, 1:2]))

                        if "m2" in TAPS:
                            @block.sync
                            def _(q):
                                q.wait_ge(eV, cl["eV"])
                                dinc(q.dma_start(out=taps["m2"], in_=m2cols[:]))
                                dinc(q.dma_start(out=taps["r2"], in_=r2cols[:]))
                                dinc(q.dma_start(out=taps["sw"], in_=swcols[:]))

                        barrier()
                        rel.clear()

                    if _ph("E"):
                        with contextlib.ExitStack() as se1:
                            ee1 = se1.enter_context
                            smc = {}
                            for nm, src in (("lr", lr_c), ("li", li_c),
                                            ("pr", pr_c), ("pi", pi_c),
                                            ("qr", qr_c), ("qi", qi_c),
                                            ("ctr", ctr_c), ("cti", cti_c),
                                            ("bp", bp_c)):
                                smc[nm] = ee1(nc.sbuf_tensor("k_" + nm,
                                                             [64, 1], F32))
                            vcrf = ee1(nc.sbuf_tensor("vcrf", [64, 64], F32))
                            vcif = ee1(nc.sbuf_tensor("vcif", [64, 64], F32))
                            lsf = ee1(nc.sbuf_tensor("lsf", [1, 1], F32))
                            twrs = ee1(nc.sbuf_tensor("twrs", [64, 64], F32))
                            twis = ee1(nc.sbuf_tensor("twis", [64, 64], F32))
                            wrs = ee1(nc.sbuf_tensor("wrs", [64, 64], F32))
                            wis = ee1(nc.sbuf_tensor("wis", [64, 64], F32))
                            ccrs = ee1(nc.sbuf_tensor("ccrs", [64, 64], F32))
                            ccis = ee1(nc.sbuf_tensor("ccis", [64, 64], F32))
                            in64b = ee1(nc.sbuf_tensor("in64b", [128, 4096], BF16))
                            e1b = ee1(nc.sbuf_tensor("e1b", [128, 4096], BF16))
                            w64b = {nm: ee1(nc.sbuf_tensor("s" + nm, [128, 64],
                                                           BF16))
                                    for nm in w64c}
                            pads = {}
                            for nm in ("vcrh", "vcrl", "vcih", "vcil",
                                       "gtrh", "gtrl", "gtih", "gtil",
                                       "lrh", "lrl", "lih", "lil",
                                       "zrh", "zrl", "zih", "zil",
                                       "btrh", "btrl", "btih", "btil"):
                                pads[nm] = ee1(nc.sbuf_tensor("p" + nm,
                                                              [128, 64], BF16))
                            bph = ee1(nc.sbuf_tensor("bph", [128, 1], BF16))
                            bpl = ee1(nc.sbuf_tensor("bpl", [128, 1], BF16))
                            tsph = ee1(nc.sbuf_tensor("tsph", [128, 1], BF16))
                            vvh = ee1(nc.sbuf_tensor("vvh", [128, 8, 64], BF16))
                            vvl = ee1(nc.sbuf_tensor("vvl", [128, 8, 64], BF16))
                            pf64 = ee1(nc.sbuf_tensor("pf64", [128, 64], F32))
                            pg64 = ee1(nc.sbuf_tensor("pg64", [128, 64], F32))
                            t64a = ee1(nc.sbuf_tensor("t64a", [64, 64], F32))
                            t64b = ee1(nc.sbuf_tensor("t64b", [64, 64], F32))
                            t64c = ee1(nc.sbuf_tensor("t64c", [64, 64], F32))
                            t64d = ee1(nc.sbuf_tensor("t64d", [64, 64], F32))
                            kg = {nm: ee1(nc.sbuf_tensor("kg" + nm, [64, 64],
                                                         F32))
                                  for nm in ("00r", "00i", "01r", "01i",
                                             "10r", "10i", "11r", "11i")}
                            zre = ee1(nc.sbuf_tensor("zre", [64, 64], F32))
                            zim = ee1(nc.sbuf_tensor("zim", [64, 64], F32))
                            grf = ee1(nc.sbuf_tensor("grf", [64, 64], F32))
                            gif = ee1(nc.sbuf_tensor("gif", [64, 64], F32))
                            v8 = ee1(nc.sbuf_tensor("v8", [64, 8], F32))
                            c1 = ee1(nc.sbuf_tensor("kc1", [64, 1], F32))
                            c2 = ee1(nc.sbuf_tensor("kc2", [64, 1], F32))
                            stp = ee1(nc.sbuf_tensor("stp", [1, 1], F32))
                            tos = ee1(nc.sbuf_tensor("tos", [1, 1], F32))
                            th1 = ee1(nc.sbuf_tensor("th1", [1, 1], BF16))
                            tf1 = ee1(nc.sbuf_tensor("tf1", [1, 1], F32))
                            tl1 = ee1(nc.sbuf_tensor("tl1", [1, 1], BF16))
                            tosc = ee1(nc.sbuf_tensor("tosc", [64, 1], F32))
                            ch3 = {nm: ee1(nc.sbuf_tensor("ch" + nm,
                                                          [64, 8, 64], F32))
                                   for nm in ("ggr", "ggi", "lrr", "lir",
                                              "dre", "dim", "mag", "rrb",
                                              "dib", "t3a", "t3b", "vrr",
                                              "vri")}
                            kb = ee1(nc.sbuf_tensor("kbE", [64, 64], BF16))
                            zrow = ee1(nc.sbuf_tensor("zrow", [1, 512], BF16))

                            @block.sync
                            def _(q):
                                for nm, src in (("lr", lr_c), ("li", li_c),
                                                ("pr", pr_c), ("pi", pi_c),
                                                ("qr", qr_c), ("qi", qi_c),
                                                ("ctr", ctr_c),
                                                ("cti", cti_c),
                                                ("bp", bp_c)):
                                    dinc(q.dma_start(out=smc[nm][:], in_=src))
                                dinc(q.dma_start(out=vcrf[:], in_=vcrT))
                                dinc(q.dma_start(out=vcif[:], in_=vciT))
                                dinc(q.dma_start(out=lsf[:], in_=lstep))
                                dinc(q.dma_start(out=twrs[:], in_=twr))
                                dinc(q.dma_start(out=twis[:], in_=twi))
                                dinc(q.dma_start(out=wrs[:], in_=wre))
                                dinc(q.dma_start(out=wis[:], in_=wim))
                                dinc(q.dma_start(out=ccrs[:], in_=ccre))
                                dinc(q.dma_start(out=ccis[:], in_=ccim))

                            @block.gpsimd
                            def _(q):
                                ginc(q.dma_start(out=in64b[:], in_=in64))
                                ginc(q.dma_start(out=e1b[:], in_=e1c))
                                for nm in w64b:
                                    ginc(q.dma_start(out=w64b[nm][:],
                                                     in_=w64c[nm]))

                            @block.vector
                            def _(v):
                                for nm in pads:
                                    v.memset(pads[nm][:], 0.0)
                                v.memset(bph[:], 0.0)
                                v.memset(bpl[:], 0.0)
                                v.memset(tsph[:], 0.0)
                                v.memset(vvh[:], 0.0)
                                v.memset(vvl[:], 0.0)
                                v.memset(pf64[:], 0.0)
                                v.memset(pg64[:], 0.0)
                                c = vinc(v.memset(zrow[:], 0.0))
                                v.wait_ge(eV, c)

                            @block.sync
                            def _(q):
                                q.wait_ge(eV, cl["eV"])
                                for zz in range(10):
                                    dinc(q.dma_start(
                                        out=bass.AP(zp_d.tensor, zz * 512,
                                                    [[1, 1], [1, 512]]),
                                        in_=zrow[:]))

                            @block.scalar
                            def _(sc):
                                sc.wait_ge(dS, cl["dS"])
                                sinc(sc.activation(stp[:], lsf[:], ACT.Exp))

                            @block.vector
                            def _(v):
                                v.wait_ge(eS, cl["eS"])
                                VS(v, v.reciprocal(tos[:], stp[:]))
                                VS(v, v.tensor_scalar_mul(tos[:], tos[:], 2.0))
                                hilo(v, tos[:], th1[:], tl1[:], tf1[:])
                                VS(v, v.tensor_copy(tsph[0:1, :], th1[:]))
                                VS(v, v.tensor_copy(tsph[32:33, :], tl1[:]))
                                hilo(v, vcrf[:], pads["vcrh"][0:64, :],
                                     pads["vcrl"][0:64, :], t64a[:])
                                hilo(v, vcif[:], pads["vcih"][0:64, :],
                                     pads["vcil"][0:64, :], t64a[:])
                                hilo(v, smc["bp"][:], bph[0:64, :],
                                     bpl[0:64, :], c1[:])

                            @block.tensor
                            def _(t):
                                t.wait_ge(eV, cl["eV"])
                                t.wait_ge(dG, cl["dG"])
                                awaitr(t, "pE")
                                t.matmul(pE[0:64, 0:1],
                                         lhsT=ones128b[:, 0:64], rhs=tsph[:],
                                         start=True, stop=True)
                                t.matmul(pE[0:64, 1:2], lhsT=pads["vcrh"][:],
                                         rhs=bph[:], start=True, stop=False)
                                t.matmul(pE[0:64, 1:2], lhsT=pads["vcrl"][:],
                                         rhs=bph[:], start=False, stop=False)
                                t.matmul(pE[0:64, 1:2], lhsT=pads["vcrh"][:],
                                         rhs=bpl[:], start=False, stop=True)
                                t.matmul(pE[0:64, 2:3], lhsT=pads["vcih"][:],
                                         rhs=bph[:], start=True, stop=False)
                                t.matmul(pE[0:64, 2:3], lhsT=pads["vcil"][:],
                                         rhs=bph[:], start=False, stop=False)
                                tinc(t.matmul(pE[0:64, 2:3],
                                              lhsT=pads["vcih"][:], rhs=bpl[:],
                                              start=False, stop=True))

                            @block.vector
                            def _(v):
                                v.wait_ge(eT, cl["eT"])
                                v.tensor_copy(tosc[:], pE[0:64, 0:1])
                                v.tensor_copy(c1[:], pE[0:64, 1:2])
                                c = vinc(v.tensor_copy(c2[:], pE[0:64, 2:3]))
                                setrel("pE", "eV", c)
                                v.wait_ge(eV, c)
                                VS(v, v.tensor_scalar(grf[:], wrs[:], tosc[:],
                                                      None, ALU.mult))
                                VS(v, v.tensor_scalar(gif[:], wis[:], tosc[:],
                                                      None, ALU.mult))
                                def cmulc(v, ar, ai, br_, bi_, kre, kim):
                                    VS(v, v.tensor_tensor(t64a[0:64, 0:1],
                                                          ar, br_, ALU.mult))
                                    VS(v, v.tensor_tensor(t64b[0:64, 0:1],
                                                          ai, bi_, ALU.mult))
                                    VS(v, v.tensor_tensor(v8[:, kre:kre + 1],
                                                          t64a[0:64, 0:1],
                                                          t64b[0:64, 0:1],
                                                          ALU.add))
                                    VS(v, v.tensor_tensor(t64a[0:64, 0:1],
                                                          ar, bi_, ALU.mult))
                                    VS(v, v.tensor_tensor(t64b[0:64, 0:1],
                                                          ai, br_, ALU.mult))
                                    VS(v, v.tensor_tensor(v8[:, kim:kim + 1],
                                                          t64a[0:64, 0:1],
                                                          t64b[0:64, 0:1],
                                                          ALU.subtract))
                                cmulc(v, smc["ctr"][:], smc["cti"][:],
                                      c1[:], c2[:], 0, 1)
                                cmulc(v, smc["ctr"][:], smc["cti"][:],
                                      smc["pr"][:], smc["pi"][:], 2, 3)
                                cmulc(v, smc["qr"][:], smc["qi"][:],
                                      c1[:], c2[:], 4, 5)
                                cmulc(v, smc["qr"][:], smc["qi"][:],
                                      smc["pr"][:], smc["pi"][:], 6, 7)
                                VS(v, v.tensor_scalar(pf64[0:64, :],
                                                      onesf[0:64, 0:64],
                                                      smc["lr"][:], None,
                                                      ALU.mult))
                                hilo(v, pf64[0:64, :], pads["lrh"][0:64, :],
                                     pads["lrl"][0:64, :], t64a[:])
                                VS(v, v.tensor_scalar(pf64[0:64, :],
                                                      onesf[0:64, 0:64],
                                                      smc["li"][:], None,
                                                      ALU.mult))
                                hilo(v, pf64[0:64, :], pads["lih"][0:64, :],
                                     pads["lil"][0:64, :], t64a[:])
                                for kk in range(8):
                                    VS(v, v.tensor_scalar(pf64[0:64, :],
                                                          onesf[0:64, 0:64],
                                                          v8[:, kk:kk + 1],
                                                          None, ALU.mult))
                                    hilo(v, pf64[0:64, :], vvh[0:64, kk, :],
                                         vvl[0:64, kk, :], t64a[:])
                                VS(v, v.tensor_copy(pg64[0:64, :], grf[:]))

                            @block.tensor
                            def _(t):
                                t.wait_ge(eV, cl["eV"])
                                awaitr(t, "pA")
                                tinc(t.transpose(pA[0:64, 0:128], pg64[:],
                                                 idf[:]))

                            @block.vector
                            def _(v):
                                v.wait_ge(eT, cl["eT"])
                                c = vinc(v.tensor_copy(pf64[0:64, :],
                                                       pA[0:64, 0:64]))
                                setrel("pA", "eV", c)
                                v.wait_ge(eV, c)
                                hilo(v, pf64[0:64, :], pads["gtrh"][0:64, :],
                                     pads["gtrl"][0:64, :], t64a[:])
                                VS(v, v.tensor_copy(pg64[0:64, :], gif[:]))

                            @block.tensor
                            def _(t):
                                t.wait_ge(eV, cl["eV"])
                                awaitr(t, "pA")
                                tinc(t.transpose(pA[0:64, 0:128], pg64[:],
                                                 idf[:]))

                            @block.vector
                            def _(v):
                                v.wait_ge(eT, cl["eT"])
                                c = vinc(v.tensor_copy(pf64[0:64, :],
                                                       pA[0:64, 0:64]))
                                setrel("pA", "eV", c)
                                v.wait_ge(eV, c)
                                hilo(v, pf64[0:64, :], pads["gtih"][0:64, :],
                                     pads["gtil"][0:64, :], t64a[:])

                            for ch in range(8):
                                @block.tensor
                                def _(t, ch=ch):
                                    t.wait_ge(eV, cl["eV"])
                                    awaitr(t, "pA"); awaitr(t, "pB")
                                    sl = (ch * 512, (ch + 1) * 512)
                                    t.matmul(pA[0:64, :], lhsT=pads["gtrh"][:],
                                             rhs=e1b[:, sl[0]:sl[1]],
                                             start=True, stop=False)
                                    t.matmul(pA[0:64, :], lhsT=pads["gtrl"][:],
                                             rhs=e1b[:, sl[0]:sl[1]],
                                             start=False, stop=True)
                                    t.matmul(pB[0:64, :], lhsT=pads["gtih"][:],
                                             rhs=e1b[:, sl[0]:sl[1]],
                                             start=True, stop=False)
                                    t.matmul(pB[0:64, :], lhsT=pads["gtil"][:],
                                             rhs=e1b[:, sl[0]:sl[1]],
                                             start=False, stop=True)
                                    t.matmul(pC[0:64, :], lhsT=pads["lrh"][:],
                                             rhs=in64b[:, sl[0]:sl[1]],
                                             start=True, stop=False)
                                    t.matmul(pC[0:64, :], lhsT=pads["lrl"][:],
                                             rhs=in64b[:, sl[0]:sl[1]],
                                             start=False, stop=True)
                                    t.matmul(pD[0:64, :], lhsT=pads["lih"][:],
                                             rhs=in64b[:, sl[0]:sl[1]],
                                             start=True, stop=False)
                                    ii = t.matmul(pD[0:64, :],
                                                  lhsT=pads["lil"][:],
                                                  rhs=in64b[:, sl[0]:sl[1]],
                                                  start=False, stop=True)
                                    tinc(ii)

                                @block.vector
                                def _(v, ch=ch):
                                    v.wait_ge(eT, cl["eT"])
                                    for j in range(8):
                                        js = (j * 64, (j + 1) * 64)
                                        v.tensor_copy(ch3["ggr"][:, j, :],
                                                      pA[0:64, js[0]:js[1]])
                                        v.tensor_copy(ch3["ggi"][:, j, :],
                                                      pB[0:64, js[0]:js[1]])
                                        v.tensor_copy(ch3["lrr"][:, j, :],
                                                      pC[0:64, js[0]:js[1]])
                                        ii = v.tensor_copy(
                                            ch3["lir"][:, j, :],
                                            pD[0:64, js[0]:js[1]])
                                    c = vinc(ii)
                                    setrel("pA", "eV", c)
                                    setrel("pB", "eV", c)
                                    setrel("pC", "eV", c)
                                    setrel("pD", "eV", c)
                                    v.wait_ge(eV, c)
                                    VS(v, v.tensor_tensor(ch3["dre"][:],
                                                          ch3["ggr"][:],
                                                          ch3["lrr"][:],
                                                          ALU.subtract))
                                    VS(v, v.tensor_tensor(ch3["dim"][:],
                                                          ch3["ggi"][:],
                                                          ch3["lir"][:],
                                                          ALU.subtract))
                                    VS(v, v.tensor_tensor(ch3["t3a"][:],
                                                          ch3["dre"][:],
                                                          ch3["dre"][:],
                                                          ALU.mult))
                                    VS(v, v.tensor_tensor(ch3["t3b"][:],
                                                          ch3["dim"][:],
                                                          ch3["dim"][:],
                                                          ALU.mult))
                                    VS(v, v.tensor_tensor(ch3["mag"][:],
                                                          ch3["t3a"][:],
                                                          ch3["t3b"][:],
                                                          ALU.add))
                                    VS(v, v.reciprocal(ch3["mag"][:],
                                                       ch3["mag"][:]))
                                    VS(v, v.tensor_tensor(ch3["rrb"][:],
                                                          ch3["dre"][:],
                                                          ch3["mag"][:],
                                                          ALU.mult))
                                    VS(v, v.tensor_tensor(ch3["dib"][:],
                                                          ch3["dim"][:],
                                                          ch3["mag"][:],
                                                          ALU.mult))

                                for kk, nm in enumerate(("00", "01", "10",
                                                         "11")):
                                    @block.tensor
                                    def _(t, ch=ch, kk=kk):
                                        t.wait_ge(eV, cl["eV"])
                                        awaitr(t, "pA"); awaitr(t, "pB")
                                        sl = (ch * 512, (ch + 1) * 512)
                                        t.matmul(pA[0:64, :],
                                                 lhsT=vvh[:, kk * 2, :],
                                                 rhs=in64b[:, sl[0]:sl[1]],
                                                 start=True, stop=False)
                                        t.matmul(pA[0:64, :],
                                                 lhsT=vvl[:, kk * 2, :],
                                                 rhs=in64b[:, sl[0]:sl[1]],
                                                 start=False, stop=True)
                                        t.matmul(pB[0:64, :],
                                                 lhsT=vvh[:, kk * 2 + 1, :],
                                                 rhs=in64b[:, sl[0]:sl[1]],
                                                 start=True, stop=False)
                                        ii = t.matmul(
                                            pB[0:64, :],
                                            lhsT=vvl[:, kk * 2 + 1, :],
                                            rhs=in64b[:, sl[0]:sl[1]],
                                            start=False, stop=True)
                                        tinc(ii)

                                    @block.vector
                                    def _(v, ch=ch, kk=kk, nm=nm):
                                        v.wait_ge(eT, cl["eT"])
                                        for j in range(8):
                                            js = (j * 64, (j + 1) * 64)
                                            v.tensor_copy(
                                                ch3["vrr"][:, j, :],
                                                pA[0:64, js[0]:js[1]])
                                            ii = v.tensor_copy(
                                                ch3["vri"][:, j, :],
                                                pB[0:64, js[0]:js[1]])
                                        c = vinc(ii)
                                        setrel("pA", "eV", c)
                                        setrel("pB", "eV", c)
                                        v.wait_ge(eV, c)
                                        VS(v, v.tensor_tensor(ch3["t3a"][:],
                                                              ch3["vrr"][:],
                                                              ch3["rrb"][:],
                                                              ALU.mult))
                                        VS(v, v.tensor_tensor(ch3["t3b"][:],
                                                              ch3["vri"][:],
                                                              ch3["dib"][:],
                                                              ALU.mult))
                                        VS(v, v.tensor_tensor(ch3["t3a"][:],
                                                              ch3["t3a"][:],
                                                              ch3["t3b"][:],
                                                              ALU.add))
                                        VS(v, v.reduce_sum(
                                            kg[nm + "r"][:, ch * 8:(ch + 1) * 8],
                                            ch3["t3a"][:], axis=AX))
                                        VS(v, v.tensor_tensor(ch3["t3a"][:],
                                                              ch3["vri"][:],
                                                              ch3["rrb"][:],
                                                              ALU.mult))
                                        VS(v, v.tensor_tensor(ch3["t3b"][:],
                                                              ch3["vrr"][:],
                                                              ch3["dib"][:],
                                                              ALU.mult))
                                        VS(v, v.tensor_tensor(ch3["t3a"][:],
                                                              ch3["t3a"][:],
                                                              ch3["t3b"][:],
                                                              ALU.subtract))
                                        VS(v, v.reduce_sum(
                                            kg[nm + "i"][:, ch * 8:(ch + 1) * 8],
                                            ch3["t3a"][:], axis=AX))

                            @block.vector
                            def _(v):
                                VS(v, v.tensor_scalar_add(t64a[:], kg["11r"][:],
                                                          1.0))
                                VS(v, v.tensor_tensor(t64b[:], t64a[:],
                                                      t64a[:], ALU.mult))
                                VS(v, v.tensor_tensor(t64c[:], kg["11i"][:],
                                                      kg["11i"][:], ALU.mult))
                                VS(v, v.tensor_tensor(t64b[:], t64b[:],
                                                      t64c[:], ALU.add))
                                VS(v, v.reciprocal(t64b[:], t64b[:]))
                                VS(v, v.tensor_tensor(t64a[:], t64a[:],
                                                      t64b[:], ALU.mult))
                                VS(v, v.tensor_tensor(t64c[:], kg["11i"][:],
                                                      t64b[:], ALU.mult))
                                VS(v, v.tensor_tensor(t64b[:], kg["01r"][:],
                                                      t64a[:], ALU.mult))
                                VS(v, v.tensor_tensor(t64d[:], kg["01i"][:],
                                                      t64c[:], ALU.mult))
                                VS(v, v.tensor_tensor(t64b[:], t64b[:],
                                                      t64d[:], ALU.add))
                                VS(v, v.tensor_tensor(t64d[:], kg["01i"][:],
                                                      t64a[:], ALU.mult))
                                VS(v, v.tensor_tensor(t64a[:], kg["01r"][:],
                                                      t64c[:], ALU.mult))
                                VS(v, v.tensor_tensor(t64d[:], t64d[:],
                                                      t64a[:], ALU.subtract))
                                VS(v, v.tensor_tensor(t64a[:], t64b[:],
                                                      kg["10r"][:], ALU.mult))
                                VS(v, v.tensor_tensor(t64c[:], t64d[:],
                                                      kg["10i"][:], ALU.mult))
                                VS(v, v.tensor_tensor(t64a[:], t64a[:],
                                                      t64c[:], ALU.subtract))
                                VS(v, v.tensor_tensor(t64c[:], t64b[:],
                                                      kg["10i"][:], ALU.mult))
                                VS(v, v.tensor_tensor(t64b[:], t64d[:],
                                                      kg["10r"][:], ALU.mult))
                                VS(v, v.tensor_tensor(t64c[:], t64c[:],
                                                      t64b[:], ALU.add))
                                VS(v, v.tensor_tensor(t64a[:], kg["00r"][:],
                                                      t64a[:], ALU.subtract))
                                VS(v, v.tensor_tensor(t64c[:], kg["00i"][:],
                                                      t64c[:], ALU.subtract))
                                VS(v, v.tensor_tensor(zre[:], ccrs[:],
                                                      t64a[:], ALU.mult))
                                VS(v, v.tensor_tensor(t64b[:], ccis[:],
                                                      t64c[:], ALU.mult))
                                VS(v, v.tensor_tensor(zre[:], zre[:],
                                                      t64b[:], ALU.subtract))
                                VS(v, v.tensor_tensor(zim[:], ccrs[:],
                                                      t64c[:], ALU.mult))
                                VS(v, v.tensor_tensor(t64b[:], ccis[:],
                                                      t64a[:], ALU.mult))
                                VS(v, v.tensor_tensor(zim[:], zim[:],
                                                      t64b[:], ALU.add))
                                VS(v, v.tensor_copy(pf64[0:64, :], zre[:]))
                                hilo(v, pf64[0:64, :], pads["zrh"][0:64, :],
                                     pads["zrl"][0:64, :], t64a[:])
                                VS(v, v.tensor_copy(pf64[0:64, :], zim[:]))
                                hilo(v, pf64[0:64, :], pads["zih"][0:64, :],
                                     pads["zil"][0:64, :], t64a[:])

                            @block.tensor
                            def _(t):
                                t.wait_ge(eV, cl["eV"])
                                awaitr(t, "pA"); awaitr(t, "pB")
                                for lh, rh in (("w64rh", "zrh"), ("w64rl", "zrh"),
                                               ("w64rh", "zrl"), ("w64nh", "zih"),
                                               ("w64nl", "zih"), ("w64nh", "zil")):
                                    t.matmul(pA[0:64, 0:64], lhsT=w64b[lh][:],
                                             rhs=pads[rh][:],
                                             start=(lh == "w64rh" and rh == "zrh"),
                                             stop=(lh == "w64nh" and rh == "zil"))
                                for lh, rh in (("w64rh", "zih"), ("w64rl", "zih"),
                                               ("w64rh", "zil"), ("w64ih", "zrh"),
                                               ("w64il", "zrh"), ("w64ih", "zrl")):
                                    ii = t.matmul(
                                        pB[0:64, 0:64], lhsT=w64b[lh][:],
                                        rhs=pads[rh][:],
                                        start=(lh == "w64rh" and rh == "zih"),
                                        stop=(lh == "w64ih" and rh == "zrl"))
                                tinc(ii)

                            @block.vector
                            def _(v):
                                v.wait_ge(eT, cl["eT"])
                                v.tensor_copy(t64a[:], pA[0:64, 0:64])
                                c = vinc(v.tensor_copy(t64b[:],
                                                       pB[0:64, 0:64]))
                                setrel("pA", "eV", c); setrel("pB", "eV", c)
                                v.wait_ge(eV, c)
                                VS(v, v.tensor_tensor(t64c[:], t64a[:],
                                                      twrs[:], ALU.mult))
                                VS(v, v.tensor_tensor(t64d[:], t64b[:],
                                                      twis[:], ALU.mult))
                                VS(v, v.tensor_tensor(t64c[:], t64c[:],
                                                      t64d[:], ALU.subtract))
                                VS(v, v.tensor_tensor(t64d[:], t64a[:],
                                                      twis[:], ALU.mult))
                                VS(v, v.tensor_tensor(zre[:], t64b[:],
                                                      twrs[:], ALU.mult))
                                VS(v, v.tensor_tensor(t64d[:], t64d[:],
                                                      zre[:], ALU.add))
                                VS(v, v.tensor_copy(pf64[0:64, :], t64c[:]))
                                VS(v, v.tensor_copy(pg64[0:64, :], t64d[:]))

                            @block.tensor
                            def _(t):
                                t.wait_ge(eV, cl["eV"])
                                awaitr(t, "pA"); awaitr(t, "pB")
                                t.transpose(pA[0:64, 0:128], pf64[:], idf[:])
                                tinc(t.transpose(pB[0:64, 0:128], pg64[:],
                                                 idf[:]))

                            @block.vector
                            def _(v):
                                v.wait_ge(eT, cl["eT"])
                                v.tensor_copy(t64a[:], pA[0:64, 0:64])
                                c = vinc(v.tensor_copy(t64b[:],
                                                       pB[0:64, 0:64]))
                                setrel("pA", "eV", c); setrel("pB", "eV", c)
                                v.wait_ge(eV, c)
                                VS(v, v.tensor_copy(pf64[0:64, :], t64a[:]))
                                hilo(v, pf64[0:64, :], pads["btrh"][0:64, :],
                                     pads["btrl"][0:64, :], t64c[:])
                                VS(v, v.tensor_copy(pf64[0:64, :], t64b[:]))
                                hilo(v, pf64[0:64, :], pads["btih"][0:64, :],
                                     pads["btil"][0:64, :], t64c[:])

                            @block.tensor
                            def _(t):
                                t.wait_ge(eV, cl["eV"])
                                awaitr(t, "pE")
                                for ix, (lh, rh) in enumerate(
                                        (("w64rh", "btrh"), ("w64rl", "btrh"),
                                         ("w64rh", "btrl"), ("w64nh", "btih"),
                                         ("w64nl", "btih"), ("w64nh", "btil"))):
                                    ii = t.matmul(pE[0:64, 0:64],
                                                  lhsT=w64b[lh][:],
                                                  rhs=pads[rh][:],
                                                  start=(ix == 0),
                                                  stop=(ix == 5))
                                tinc(ii)

                            @block.vector
                            def _(v):
                                v.wait_ge(eT, cl["eT"])
                                c = vinc(v.tensor_copy(kb[:], pE[0:64, 0:64]))
                                setrel("pE", "eV", c)

                            @block.sync
                            def _(q):
                                q.wait_ge(eV, cl["eV"])
                                dinc(q.dma_start(
                                    out=bass.AP(zp_d.tensor, 512,
                                                [[64, 64], [1, 64]]),
                                    in_=kb[:]))
                                if "kb" in TAPS:
                                    dinc(q.dma_start(out=taps["kb"],
                                                     in_=kb[:]))

                            barrier()
                            rel.clear()

                    if _ph("E"):
                        with contextlib.ExitStack() as se2:
                            ee2 = se2.enter_context
                            convW = ee2(nc.sbuf_tensor("convW",
                                                       [128, 36, 512], BF16))
                            hbuf = ee2(nc.sbuf_tensor("hbuf", [128, 2, 512],
                                                      BF16))

                            for ix in range(36):
                                d128 = ix - 3
                                off = 512 + 128 * d128 - 127
                                slot = ix % 2

                                @block.gpsimd
                                def _(q, ix=ix, off=off, slot=slot):
                                    awaitr(q, f"hb{slot}")
                                    c = ginc(q.dma_start(
                                        out=hbuf[:, slot, :],
                                        in_=bass.AP(zp_d.tensor, off,
                                                    [[1, 128], [1, 512]])))
                                    rel[f"hbd{slot}"] = ("dG", c)

                                @block.tensor
                                def _(t, ix=ix, slot=slot):
                                    t.wait_ge(dG, rel[f"hbd{slot}"][1])
                                    awaitr(t, "pE")
                                    c = tinc(t.matmul(pE[:], lhsT=jbuf[:],
                                                      rhs=hbuf[:, slot, :],
                                                      start=True, stop=True))
                                    setrel(f"hb{slot}", "eT", c)

                                @block.vector
                                def _(v, ix=ix):
                                    v.wait_ge(eT, cl["eT"])
                                    c = vinc(v.tensor_copy(convW[:, ix, :],
                                                           pE[:]))
                                    setrel("pE", "eV", c)

                            for ti in range(8):
                                slot = ti % 2

                                @block.sync
                                def _(q, ti=ti, slot=slot):
                                    awaitr(q, f"xb{slot}")
                                    c = dinc(q.dma_start(
                                        out=xbufD[:, slot, :],
                                        in_=bass.AP(x1_d.tensor,
                                                    ti * 128 * D,
                                                    [[D, 128], [1, D]])))
                                    rel[f"xdma{slot}"] = ("dS", c)

                                for ich in range(8):
                                    bk = banks[ich % 2]
                                    bn = bnames[ich % 2]
                                    jlo = max(0, 4 * ich - 32)
                                    jhi = min(31, 4 * ich + 3)

                                    @block.tensor
                                    def _(t, ti=ti, ich=ich, bk=bk, bn=bn,
                                          jlo=jlo, jhi=jhi):
                                        awaitr(t, bn)
                                        if ti == 0 and ich == 0:
                                            t.wait_ge(eV, cl["eV"])
                                        for jt in range(jlo, jhi + 1):
                                            ii = t.matmul(
                                                bk[:],
                                                lhsT=xn2T[:, jt,
                                                          ti * 128:(ti + 1) * 128],
                                                rhs=convW[:, 4 * ich - jt + 3, :],
                                                start=(jt == jlo),
                                                stop=(jt == jhi))
                                        tinc(ii)

                                    @block.vector
                                    def _(v, ti=ti, ich=ich, slot=slot,
                                          bk=bk, bn=bn):
                                        v.wait_ge(eT, cl["eT"])
                                        v.wait_ge(dS, rel[f"xdma{slot}"][1])
                                        awaitr(v, "oaccb")
                                        sl = (ich * 512, (ich + 1) * 512)
                                        VS(v, v.tensor_scalar(
                                            tmp512[:, slot, :],
                                            xbufD[:, slot, sl[0]:sl[1]],
                                            a1cols[:, ti:ti + 1],
                                            bncols[:, ti:ti + 1],
                                            ALU.mult, ALU.add))
                                        c = vinc(v.tensor_tensor(
                                            oaccb[:, sl[0]:sl[1]],
                                            tmp512[:, slot, :], bk[:],
                                            ALU.add))
                                        setrel(bn, "eV", c)
                                        if ich == 7:
                                            setrel(f"xb{slot}", "eV", c)
                                            rel["oaw"] = ("eV", c)

                                @block.sync
                                def _(q, ti=ti):
                                    q.wait_ge(eV, rel["oaw"][1])
                                    c = dinc(q.dma_start(
                                        out=bass.AP(oa_d.tensor, ti * 128 * D,
                                                    [[D, 128], [1, D]]),
                                        in_=oaccb[:]))
                                    setrel("oaccb", "dS", c)

                            if "oa" in TAPS:
                                @block.sync
                                def _(q):
                                    q.wait_ge(dS, cl["dS"])
                                    dinc(q.dma_start(out=taps["oa"], in_=oa_d))

                            barrier()
                            rel.clear()

                    sde.close()

                    if _ph("F"):
                        with contextlib.ExitStack() as sf1:
                            ef1 = sf1.enter_context
                            latT = ef1(nc.sbuf_tensor("latT",
                                                      [128, 16, TPC], BF16))
                            with contextlib.ExitStack() as sf1a:
                                ef1a = sf1a.enter_context
                                latb = ef1a(nc.sbuf_tensor("latb",
                                                           [128, 8, 2048],
                                                           BF16))
                                fc1b = ef1a(nc.sbuf_tensor("fc1b",
                                                           [128, 32, 512],
                                                           BF16))

                                for nch in range(4):
                                    @block.gpsimd
                                    def _(q, nch=nch):
                                        awaitr(q, "fc1b")
                                        c = ginc(q.dma_start(
                                            out=fc1b[:],
                                            in_=bass.AP(fc1_w.tensor,
                                                        nch * 512,
                                                        [[2048, 128],
                                                         [128 * 2048, 32],
                                                         [1, 512]])))
                                        rel["fc1d"] = ("dG", c)

                                    for ti in range(8):
                                        bk = banks[ti % 2]
                                        bn = bnames[ti % 2]

                                        @block.tensor
                                        def _(t, nch=nch, ti=ti, bk=bk, bn=bn):
                                            t.wait_ge(dG, rel["fc1d"][1])
                                            awaitr(t, bn)
                                            for a in range(32):
                                                ii = t.matmul(
                                                    bk[:],
                                                    lhsT=xn2T[:, a,
                                                              ti * 128:(ti + 1) * 128],
                                                    rhs=fc1b[:, a, :],
                                                    start=(a == 0),
                                                    stop=(a == 31))
                                            c = tinc(ii)
                                            if ti == 7:
                                                setrel("fc1b", "eT", c)

                                        @block.vector
                                        def _(v, nch=nch, ti=ti, bk=bk, bn=bn):
                                            v.wait_ge(eT, cl["eT"])
                                            c = vinc(v.tensor_copy(
                                                latb[:, ti,
                                                     nch * 512:(nch + 1) * 512],
                                                bk[:]))
                                            setrel(bn, "eV", c)

                                for ti in range(8):
                                    for grp in range(4):
                                        @block.tensor
                                        def _(t, ti=ti, grp=grp):
                                            awaitr(t, "pbfq")
                                            if ti == 0 and grp == 0:
                                                t.wait_ge(eV, cl["eV"])
                                            for j in range(4):
                                                lt = grp * 4 + j
                                                ii = t.transpose(
                                                    pbf[:, j * 128:(j + 1) * 128],
                                                    latb[:, ti,
                                                         lt * 128:(lt + 1) * 128],
                                                    idb[:])
                                            tinc(ii)

                                        @block.vector
                                        def _(v, ti=ti, grp=grp):
                                            v.wait_ge(eT, cl["eT"])
                                            for j in range(4):
                                                lt = grp * 4 + j
                                                ii = v.tensor_copy(
                                                    latT[:, lt,
                                                         ti * 128:(ti + 1) * 128],
                                                    pbf[:, j * 128:(j + 1) * 128])
                                            setrel("pbfq", "eV", vinc(ii))

                                barrier()
                                rel.clear()

                            with contextlib.ExitStack() as sf1b:
                                ef1b = sf1b.enter_context
                                expb = ef1b(nc.sbuf_tensor("expb",
                                                           [128, 2, 16, 512],
                                                           BF16))
                                moeb = ef1b(nc.sbuf_tensor("moeb",
                                                           [128, 8, 2048],
                                                           BF16))
                                motl = ef1b(nc.sbuf_tensor("motl",
                                                           [128, 16, 128],
                                                           BF16))

                                for nch in range(4):
                                    slot = nch % 2

                                    @block.gpsimd
                                    def _(q, nch=nch, slot=slot):
                                        awaitr(q, f"expb{slot}")
                                        c = ginc(q.dma_start(
                                            out=expb[:, slot, :, :],
                                            in_=bass.AP(exp_w.tensor,
                                                        nch * 512,
                                                        [[2048, 128],
                                                         [128 * 2048, 16],
                                                         [1, 512]])))
                                        rel[f"expd{slot}"] = ("dG", c)

                                    for ti in range(8):
                                        bk = banks[ti % 2]
                                        bn = bnames[ti % 2]

                                        @block.tensor
                                        def _(t, nch=nch, ti=ti, slot=slot,
                                              bk=bk, bn=bn):
                                            t.wait_ge(dG, rel[f"expd{slot}"][1])
                                            awaitr(t, bn)
                                            for lt in range(16):
                                                ii = t.matmul(
                                                    bk[:],
                                                    lhsT=latT[:, lt,
                                                              ti * 128:(ti + 1) * 128],
                                                    rhs=expb[:, slot, lt, :],
                                                    start=(lt == 0),
                                                    stop=(lt == 15))
                                            c = tinc(ii)
                                            if ti == 7:
                                                setrel(f"expb{slot}", "eT", c)

                                        @block.vector
                                        def _(v, nch=nch, ti=ti, bk=bk, bn=bn):
                                            v.wait_ge(eT, cl["eT"])
                                            c = vinc(v.tensor_scalar(
                                                moeb[:, ti,
                                                     nch * 512:(nch + 1) * 512],
                                                bk[:], swcols[:, ti:ti + 1],
                                                None, ALU.mult))
                                            setrel(bn, "eV", c)

                                for ti in range(8):
                                    for grp in range(4):
                                        @block.tensor
                                        def _(t, ti=ti, grp=grp):
                                            awaitr(t, "pbfq")
                                            if grp == 0:
                                                t.wait_ge(eV, cl["eV"])
                                            for j in range(4):
                                                lt = grp * 4 + j
                                                ii = t.transpose(
                                                    pbf[:, j * 128:(j + 1) * 128],
                                                    moeb[:, ti,
                                                         lt * 128:(lt + 1) * 128],
                                                    idb[:])
                                            tinc(ii)

                                        @block.vector
                                        def _(v, ti=ti, grp=grp):
                                            v.wait_ge(eT, cl["eT"])
                                            awaitr(v, "motl")
                                            for j in range(4):
                                                lt = grp * 4 + j
                                                ii = v.tensor_copy(
                                                    motl[:, lt, :],
                                                    pbf[:, j * 128:(j + 1) * 128])
                                            c = vinc(ii)
                                            setrel("pbfq", "eV", c)
                                            rel["motlw"] = ("eV", c)

                                    @block.sync
                                    def _(q, ti=ti):
                                        q.wait_ge(eV, rel["motlw"][1])
                                        c = dinc(q.dma_start(
                                            out=bass.AP(moe_d.tensor, ti * 128,
                                                        [[TPC, 128],
                                                         [128 * TPC, 16],
                                                         [1, 128]]),
                                            in_=motl[:]))
                                        setrel("motl", "dS", c)

                                barrier()
                                rel.clear()

                    if _ph("F"):
                        with contextlib.ExitStack() as sf2:
                            ef2 = sf2.enter_context
                            upb = ef2(nc.sbuf_tensor("upb", [128, 2, 32, 512],
                                                     BF16))
                            rbuf = ef2(nc.sbuf_tensor("rbuf", [128, 2, 512],
                                                      F32))
                            hbft = ef2(nc.sbuf_tensor("hbft", [128, 2, 512],
                                                      BF16))
                            hstg = ef2(nc.sbuf_tensor("hstg", [128, 2, 4, 128],
                                                      BF16))

                            for ich in range(16):
                                slot = ich % 2

                                @block.gpsimd
                                def _(q, ich=ich, slot=slot):
                                    awaitr(q, f"upb{slot}")
                                    c = ginc(q.dma_start(
                                        out=upb[:, slot, :, :],
                                        in_=bass.AP(up_w.tensor, ich * 512,
                                                    [[2 * D, 128],
                                                     [128 * 2 * D, 32],
                                                     [1, 512]])))
                                    rel[f"upd{slot}"] = ("dG", c)

                                for ti in range(8):
                                    bk = banks[ti % 2]
                                    bn = bnames[ti % 2]
                                    ts2 = ti % 2

                                    @block.tensor
                                    def _(t, ich=ich, ti=ti, slot=slot,
                                          bk=bk, bn=bn):
                                        t.wait_ge(dG, rel[f"upd{slot}"][1])
                                        awaitr(t, bn)
                                        for a in range(32):
                                            ii = t.matmul(
                                                bk[:],
                                                lhsT=xn2T[:, a,
                                                          ti * 128:(ti + 1) * 128],
                                                rhs=upb[:, slot, a, :],
                                                start=(a == 0), stop=(a == 31))
                                        c = tinc(ii)
                                        if ti == 7:
                                            setrel(f"upb{slot}", "eT", c)

                                    @block.scalar
                                    def _(sc, ich=ich, ti=ti, ts2=ts2,
                                          bk=bk, bn=bn):
                                        sc.wait_ge(eT, cl["eT"])
                                        awaitr(sc, f"rb{ts2}")
                                        c = sinc(sc.activation(
                                            rbuf[:, ts2, :], bk[:], ACT.Relu))
                                        setrel(bn, "eS", c)
                                        rel[f"rbw{ts2}"] = ("eS", c)

                                    @block.vector
                                    def _(v, ich=ich, ti=ti, ts2=ts2):
                                        v.wait_ge(eS, rel[f"rbw{ts2}"][1])
                                        awaitr(v, f"hbf{ts2}")
                                        c = vinc(v.tensor_tensor(
                                            hbft[:, ts2, :], rbuf[:, ts2, :],
                                            rbuf[:, ts2, :], ALU.mult))
                                        setrel(f"rb{ts2}", "eV", c)
                                        rel[f"hbw{ts2}"] = ("eV", c)

                                    @block.tensor
                                    def _(t, ich=ich, ti=ti, ts2=ts2):
                                        t.wait_ge(eV, rel[f"hbw{ts2}"][1])
                                        awaitr(t, "pbfq")
                                        for j in range(4):
                                            ii = t.transpose(
                                                pbf[:, j * 128:(j + 1) * 128],
                                                hbft[:, ts2,
                                                     j * 128:(j + 1) * 128],
                                                idb[:])
                                        c = tinc(ii)
                                        rel["htr"] = ("eT", c)
                                        setrel(f"hbf{ts2}", "eT", c)

                                    @block.vector
                                    def _(v, ich=ich, ti=ti, ts2=ts2):
                                        v.wait_ge(eT, rel["htr"][1])
                                        awaitr(v, f"hst{ts2}")
                                        for j in range(4):
                                            ii = v.tensor_copy(
                                                hstg[:, ts2, j, :],
                                                pbf[:, j * 128:(j + 1) * 128])
                                        c = vinc(ii)
                                        setrel("pbfq", "eV", c)
                                        rel[f"hsw{ts2}"] = ("eV", c)

                                    @block.sync
                                    def _(q, ich=ich, ti=ti, ts2=ts2):
                                        q.wait_ge(eV, rel[f"hsw{ts2}"][1])
                                        c = dinc(q.dma_start(
                                            out=bass.AP(
                                                ht_d.tensor,
                                                (ich * 4) * 128 * TPC + ti * 128,
                                                [[TPC, 128],
                                                 [128 * TPC, 4], [1, 128]]),
                                            in_=hstg[:, ts2, :, :]))
                                        setrel(f"hst{ts2}", "dS", c)

                            if "ht" in TAPS:
                                @block.sync
                                def _(q):
                                    q.wait_ge(dS, cl["dS"])
                                    dinc(q.dma_start(out=taps["ht"], in_=ht_d))

                            barrier()
                            rel.clear()

        if _ph("F"):
            with contextlib.ExitStack() as sf3:
                ef3 = sf3.enter_context
                accS = ef3(nc.sbuf_tensor("accS", [128, 8, 2048], F32))
                downb = ef3(nc.sbuf_tensor("downb", [128, 2, 8, 2048], BF16))
                htb = ef3(nc.sbuf_tensor("htb", [128, 2, 8, 128], BF16))

                for oh in range(2):
                    for grp in range(8):
                        slot = grp % 2

                        @block.gpsimd
                        def _(q, oh=oh, grp=grp, slot=slot):
                            awaitr(q, f"dwb{slot}")
                            c = ginc(q.dma_start(
                                out=downb[:, slot, :, :],
                                in_=bass.AP(down_w.tensor,
                                            grp * 1024 * D + oh * 2048,
                                            [[D, 128], [128 * D, 8],
                                             [1, 2048]])))
                            rel[f"dwd{slot}"] = ("dG", c)

                        for ti in range(8):
                            hs = ti % 2

                            @block.sync
                            def _(q, oh=oh, grp=grp, ti=ti, hs=hs):
                                awaitr(q, f"htb{hs}")
                                c = dinc(q.dma_start(
                                    out=htb[:, hs, :, :],
                                    in_=bass.AP(ht_d.tensor,
                                                (grp * 8) * 128 * TPC + ti * 128,
                                                [[TPC, 128], [128 * TPC, 8],
                                                 [1, 128]])))
                                rel[f"htd{hs}"] = ("dS", c)

                            @block.tensor
                            def _(t, oh=oh, grp=grp, ti=ti, hs=hs, slot=slot):
                                t.wait_ge(dG, rel[f"dwd{slot}"][1])
                                t.wait_ge(dS, rel[f"htd{hs}"][1])
                                for och in range(4):
                                    bk = banks[och]
                                    awaitr(t, bnames[och])
                                    for j in range(8):
                                        ii = t.matmul(
                                            bk[:],
                                            lhsT=htb[:, hs, j, :],
                                            rhs=downb[:, slot, j,
                                                      och * 512:(och + 1) * 512],
                                            start=(j == 0), stop=(j == 7))
                                    c = tinc(ii)
                                    rel[f"dmm{och}"] = ("eT", c)
                                setrel(f"htb{hs}", "eT", c)
                                if ti == 7:
                                    setrel(f"dwb{slot}", "eT", c)

                            @block.vector
                            def _(v, oh=oh, grp=grp, ti=ti):
                                awaitr(v, f"accg{ti}")
                                for och in range(4):
                                    v.wait_ge(eT, rel[f"dmm{och}"][1])
                                    sl = (och * 512, (och + 1) * 512)
                                    if grp == 0:
                                        ii = v.tensor_copy(
                                            accS[:, ti, sl[0]:sl[1]],
                                            banks[och][:])
                                    else:
                                        ii = v.tensor_tensor(
                                            accS[:, ti, sl[0]:sl[1]],
                                            accS[:, ti, sl[0]:sl[1]],
                                            banks[och][:], ALU.add)
                                    c = vinc(ii)
                                    setrel(bnames[och], "eV", c)
                                rel[f"accg{ti}"] = ("eV", c)

                        if STRICT in (1, 3):
                            barrier()

                    for ti in range(8):
                        @block.sync
                        def _(q, oh=oh, ti=ti):
                            q.wait_ge(eV, rel[f"accg{ti}"][1])
                            dinc(q.dma_start(
                                out=bass.AP(ac_d.tensor,
                                            ti * 128 * D + oh * 2048,
                                            [[D, 128], [1, 2048]]),
                                in_=accS[:, ti, :]))

                    barrier()
                    for ti in range(8):
                        rel.pop(f"accg{ti}", None)

                if "ac" in TAPS:
                    @block.sync
                    def _(q):
                        q.wait_ge(dS, cl["dS"])
                        dinc(q.dma_start(out=taps["ac"], in_=ac_d))

                barrier()
                rel.clear()

        if _ph("F"):
            with contextlib.ExitStack() as sf4:
                ef4 = sf4.enter_context
                fA = ef4(nc.sbuf_tensor("fA", [128, D], F32))
                fB = ef4(nc.sbuf_tensor("fB", [128, D], F32))
                fC = ef4(nc.sbuf_tensor("fC", [128, D], F32))
                fD = ef4(nc.sbuf_tensor("fD", [128, D], F32))
                fc2b = ef4(nc.sbuf_tensor("fc2b", [128, 2, 16, 512], BF16))
                motl2 = ef4(nc.sbuf_tensor("motl2", [128, 2, 16, 128], BF16))
                grep = ef4(nc.sbuf_tensor("grep", [128, D], F32))
                brep = ef4(nc.sbuf_tensor("brep", [128, D], F32))
                gpb = ef4(nc.sbuf_tensor("gpb", [128, D], BF16))
                grw = ef4(nc.sbuf_tensor("grw", [1, D], F32))
                gtm = ef4(nc.sbuf_tensor("gtm", [1, D], F32))

                for which, src, dst in (("g", gam_row, grep),
                                        ("b", bet_row, brep)):
                    @block.sync
                    def _(q, src=src):
                        q.wait_ge(eV, cl["eV"])
                        dinc(q.dma_start(out=grw[:], in_=src))

                    @block.vector
                    def _(v):
                        v.wait_ge(dS, cl["dS"])
                        v.memset(gpb[:], 0.0)
                        hilo(v, grw[:], gpb[0:1, :], gpb[32:33, :], gtm[:])

                    for och in range(8):
                        @block.tensor
                        def _(t, och=och):
                            t.wait_ge(eV, cl["eV"])
                            awaitr(t, "pE")
                            tinc(t.matmul(pE[:], lhsT=ones128b[:],
                                          rhs=gpb[:, och * 512:(och + 1) * 512],
                                          start=True, stop=True))

                        @block.vector
                        def _(v, och=och, dst=dst):
                            v.wait_ge(eT, cl["eT"])
                            c = vinc(v.tensor_copy(
                                dst[:, och * 512:(och + 1) * 512], pE[:]))
                            setrel("pE", "eV", c)

                for ti in range(8):
                    ms = ti % 2

                    @block.sync
                    def _(q, ti=ti):
                        awaitr(q, "fA")
                        dinc(q.dma_start(out=fA[:], in_=bass.AP(
                            ac_d.tensor, ti * 128 * D, [[D, 128], [1, D]])))
                        awaitr(q, "fB")
                        c = dinc(q.dma_start(out=fB[:], in_=bass.AP(
                            oa_d.tensor, ti * 128 * D, [[D, 128], [1, D]])))
                        rel["fABd"] = ("dS", c)

                    @block.gpsimd
                    def _(q, ti=ti, ms=ms):
                        awaitr(q, f"mo2{ms}")
                        c = ginc(q.dma_start(
                            out=motl2[:, ms, :, :],
                            in_=bass.AP(moe_d.tensor, ti * 128,
                                        [[TPC, 128], [128 * TPC, 16],
                                         [1, 128]])))
                        rel[f"mo2d{ms}"] = ("dG", c)

                    for och in range(8):
                        slot = och % 2
                        bk = banks[och % 2]
                        bn = bnames[och % 2]

                        @block.gpsimd
                        def _(q, ti=ti, och=och, slot=slot):
                            awaitr(q, f"fc2b{slot}")
                            c = ginc(q.dma_start(
                                out=fc2b[:, slot, :, :],
                                in_=bass.AP(fc2_w.tensor, och * 512,
                                            [[D, 128], [128 * D, 16],
                                             [1, 512]])))
                            rel[f"fc2d{slot}"] = ("dG", c)

                        @block.tensor
                        def _(t, ti=ti, och=och, slot=slot, ms=ms,
                              bk=bk, bn=bn):
                            t.wait_ge(dG, rel[f"fc2d{slot}"][1])
                            t.wait_ge(dG, rel[f"mo2d{ms}"][1])
                            awaitr(t, bn)
                            for lt in range(16):
                                ii = t.matmul(bk[:],
                                              lhsT=motl2[:, ms, lt, :],
                                              rhs=fc2b[:, slot, lt, :],
                                              start=(lt == 0), stop=(lt == 15))
                            c = tinc(ii)
                            setrel(f"fc2b{slot}", "eT", c)
                            if och == 7:
                                setrel(f"mo2{ms}", "eT", c)

                        @block.vector
                        def _(v, ti=ti, och=och, bk=bk, bn=bn):
                            v.wait_ge(eT, cl["eT"])
                            v.wait_ge(dS, rel["fABd"][1])
                            awaitr(v, "fC")
                            sl = (och * 512, (och + 1) * 512)
                            c = vinc(v.tensor_tensor(fC[:, sl[0]:sl[1]],
                                                     bk[:], fA[:, sl[0]:sl[1]],
                                                     ALU.add))
                            setrel(bn, "eV", c)
                            if och == 7:
                                setrel("fA", "eV", c)
                                rel["fCw"] = ("eV", c)

                        if STRICT in (1, 2):
                            barrier()

                    @block.scalar
                    def _(sc, ti=ti):
                        sc.wait_ge(eV, rel["fCw"][1])
                        awaitr(sc, "fD")
                        c = sinc(sc.activation(fD[:], fC[:], ACT.Square))
                        rel["fDsq"] = ("eS", c)

                    @block.vector
                    def _(v, ti=ti):
                        v.wait_ge(eV, rel["fCw"][1])
                        i1 = vinc(v.reduce_sum(s1c[:], fC[:], axis=AX))
                        v.wait_ge(eS, rel["fDsq"][1])
                        i2 = vinc(v.reduce_sum(s2c[:], fD[:], axis=AX))
                        v.wait_ge(eV, i2)
                        VS(v, v.tensor_scalar_mul(v1c[:], s1c[:], R4))
                        VS(v, v.tensor_tensor(v2c[:], v1c[:], v1c[:],
                                              ALU.mult))
                        VS(v, v.tensor_scalar_mul(v3c[:], s2c[:], R4))
                        VS(v, v.tensor_tensor(v3c[:], v3c[:], v2c[:],
                                              ALU.subtract))
                        VS(v, v.tensor_scalar_add(v3c[:], v3c[:], EPS))

                    @block.scalar
                    def _(sc, ti=ti):
                        sc.wait_ge(eV, cl["eV"])
                        sinc(sc.activation(v2c[:], v3c[:], ACT.Sqrt))

                    @block.vector
                    def _(v, ti=ti):
                        v.wait_ge(eS, cl["eS"])
                        VS(v, v.reciprocal(v2c[:], v2c[:]))
                        VS(v, v.tensor_tensor(v4c[:], v1c[:], v2c[:],
                                              ALU.mult))
                        VS(v, v.tensor_scalar_mul(v4c[:], v4c[:], -1.0))
                        VS(v, v.tensor_scalar(fD[:], fC[:], v2c[:], v4c[:],
                                              ALU.mult, ALU.add))
                        setrel("fC", "eV", cl["eV"])
                        VS(v, v.tensor_tensor(fD[:], fD[:], grep[:],
                                              ALU.mult))
                        VS(v, v.tensor_tensor(fD[:], fD[:], brep[:],
                                              ALU.add))
                        c = VS(v, v.tensor_tensor(fD[:], fD[:], fB[:],
                                                  ALU.add))
                        setrel("fB", "eV", c)
                        rel["yw"] = ("eV", c)

                    @block.sync
                    def _(q, ti=ti):
                        q.wait_ge(eV, rel["yw"][1])
                        c = dinc(q.dma_start(
                            out=bass.AP(y_t.tensor, ti * 128 * D,
                                        [[D, 128], [1, D]]),
                            in_=fD[:]))
                        setrel("fD", "dS", c)

        @block.sync
        def _(q):
            q.wait_ge(dS, cl["dS"])
            q.wait_ge(dG, cl["dG"])

        @block.vector
        def _(v):
            v.wait_ge(dS, cl["dS"])

    return nc, taps


def kernel(**inputs):
    ins = host_prep(inputs)
    nc, _ = build()
    res = run_bass_kernel_spmd(nc, ins, core_ids=list(range(NCORE)))
    y = np.concatenate(
        [np.asarray(res.results[c]["y"], np.float32) for c in range(NCORE)],
        axis=0)
    return y.reshape(1, S, D)
